# version 05
# baseline (speedup 1.0000x reference)
import numpy as np
import ml_dtypes
import concourse.bass as bass
import concourse.mybir as mybir
from concourse.bass_utils import run_bass_kernel_spmd

F32 = mybir.dt.float32
BF16 = mybir.dt.bfloat16
I32 = mybir.dt.int32
AF = mybir.ActivationFunctionType
ALU = mybir.AluOpType

B = 2
T = 2048
C = 1024
H = 4096
DC = C // 128
HC = H // 128
TOK = B * T
SB = 512
NSB = TOK // SB
SCH = 256
NCH = T // SCH
LN_EPS = 1e-5
RMS_EPS = 1e-6
NS_ITERS = 12
NCORES = 8


class Cnt:
    __slots__ = ("h", "n")
    def __init__(self, handle):
        self.h = handle
        self.n = 0
    def inc(self, ins, k=1):
        ins.then_inc(self.h, k)
        self.n += k
        return ins


def build_program(debug=False):
    nc = bass.Bass("TRN2", target_bir_lowering=False)

    def din(name, shape):
        return nc.dram_tensor(name, shape, F32, kind="ExternalInput").ap()

    xT = din("xT", [C, TOK])
    w_lh = din("w_lh", [C, C])
    w_rh = din("w_rh", [C, C])
    wqT = din("wqT", [C, C])
    wvT = din("wvT", [C, C])
    wcT = din("wcT", [C, C])
    identv = din("identv", [128, DC])
    winT = din("winT", [C, C])
    waT = din("waT", [C, C])
    wbT = din("wbT", [C, C])
    wvbT = din("wvbT", [C, C])
    woT = din("woT", [C, C])
    b_al = din("b_al", [128, DC])
    b_be = din("b_be", [128, DC])
    b_v = din("b_v", [128, DC])
    w1T = din("w1T", [C, H])
    w2T = din("w2T", [C, H])
    h1T = din("h1T", [H, H])
    h2T = din("h2T", [H, H])
    w3T = din("w3T", [H, C])
    s1 = din("s1", [128, HC])
    s2 = din("s2", [128, HC])
    sh1 = din("sh1", [128, HC])
    sh2 = din("sh2", [128, HC])
    s3 = din("s3", [128, DC])

    outT = nc.dram_tensor("outT", [C, TOK], F32, kind="ExternalOutput").ap()

    def dint(name, shape, dt=F32):
        return nc.dram_tensor(name, shape, dt, kind="Internal").ap()

    x_res = dint("x_res", [C, TOK])
    w1Bd = dint("w1Bd", [HC, 128, DC, 128], BF16)
    w2Bd = dint("w2Bd", [HC, 128, DC, 128], BF16)
    h1Bd = dint("h1Bd", [HC, 128, HC, 128], BF16)
    h2Bd = dint("h2Bd", [HC, 128, HC, 128], BF16)
    w3Bd = dint("w3Bd", [DC, 128, HC, 128], BF16)
    wcBd = dint("wcBd", [DC, 128, DC, 128], BF16)
    winBd = dint("winBd", [DC, 128, DC, 128], BF16)
    waBd = dint("waBd", [DC, 128, DC, 128], BF16)
    wbBd = dint("wbBd", [DC, 128, DC, 128], BF16)
    wvBd = dint("wvBd", [DC, 128, DC, 128], BF16)
    woBd = dint("woBd", [DC, 128, DC, 128], BF16)
    qrD = dint("qrD", [C, TOK], BF16)
    qsD = dint("qsD", [C, TOK], BF16)
    vD = dint("vD", [C, TOK], BF16)

    dbg = {}
    if debug:
        for nm, shp in [("d_olh", [C, C]), ("d_orh", [C, C]),
                        ("d_x1", [C, TOK]), ("d_x2", [C, TOK]), ("d_x3", [C, TOK])]:
            dbg[nm] = nc.dram_tensor(nm, shp, F32, kind="ExternalOutput").ap()
        for nm, shp in [("d_qr", [C, TOK]), ("d_qs", [C, TOK])]:
            dbg[nm] = nc.dram_tensor(nm, shp, BF16, kind="ExternalOutput").ap()

    import contextlib
    es = contextlib.ExitStack()
    with es:
        ec = es.enter_context
        abf = ec(nc.sbuf_tensor("abf", [128, 32768], BF16))
        af32 = ec(nc.sbuf_tensor("af32", [128, 8192], F32))
        wslotA = ec(nc.sbuf_tensor("wslotA", [128, DC, C], BF16))
        wslotB = ec(nc.sbuf_tensor("wslotB", [128, DC, C], BF16))
        qr_s = ec(nc.sbuf_tensor("qr_s", [128, DC, 2 * SCH], BF16))
        qs_s = ec(nc.sbuf_tensor("qs_s", [128, DC, 2 * SCH], BF16))
        xn_s = ec(nc.sbuf_tensor("xn_s", [128, DC, SB], BF16))
        wbuf = ec(nc.sbuf_tensor("wbuf", [128, 2, HC, 128], BF16))
        x_t = ec(nc.sbuf_tensor("x_t", [128, DC, SB], F32))
        st = ec(nc.sbuf_tensor("st", [128, 2048], F32))
        sc_m = ec(nc.sbuf_tensor("sc", [128, 512], F32))
        scb = ec(nc.sbuf_tensor("scb", [128, 320], BF16))
        ones_f = ec(nc.sbuf_tensor("ones1", [128, 128], F32))
        idm = ec(nc.sbuf_tensor("idm", [128, 128], BF16))
        idmi = ec(nc.sbuf_tensor("idmi", [128, 128], I32))
        psA = ec(nc.psum_tensor("psA", [128, 4, 512], F32))
        psB = ec(nc.psum_tensor("psB", [128, 2, 512], F32))
        psT = ec(nc.psum_tensor("psT", [128, 2, 1024], BF16))
        _m_dma = ec(nc.semaphore("m_dma"))
        _m_dmo = ec(nc.semaphore("m_dmo"))
        _m_wst = ec(nc.semaphore("m_wst"))
        _m_pe = ec(nc.semaphore("m_pe"))
        _m_dve = ec(nc.semaphore("m_dve"))
        _m_act = ec(nc.semaphore("m_act"))
        _m_gp = ec(nc.semaphore("m_gp"))
        _m_sg0 = ec(nc.semaphore("m_sg0"))
        _m_sg1 = ec(nc.semaphore("m_sg1"))
        _m_w0 = ec(nc.semaphore("m_w0"))
        _m_w1 = ec(nc.semaphore("m_w1"))
        _m_so0 = ec(nc.semaphore("m_so0"))
        _m_so1 = ec(nc.semaphore("m_so1"))
        _m_ph = ec(nc.semaphore("m_ph"))
        blk = ec(nc.Block())
        dma = Cnt(_m_dma)
        dmo = Cnt(_m_dmo)
        wst = Cnt(_m_wst)
        pe = Cnt(_m_pe)
        dve = Cnt(_m_dve)
        act = Cnt(_m_act)
        gp = Cnt(_m_gp)
        ph = Cnt(_m_ph)
        sgc = [Cnt(_m_sg0), Cnt(_m_sg1)]
        wbc = [Cnt(_m_w0), Cnt(_m_w1)]
        soc = [Cnt(_m_so0), Cnt(_m_so1)]

        v_ = lambda ap, a: ap.rearrange("p (a b) -> p a b", a=a)
        Xf = v_(af32[:, 0:8192], DC)
        Xbf = v_(abf[:, 0:8192], DC)
        XTb = v_(abf[:, 8192:16384], DC)
        Gb = v_(abf[:, 16384:24576], DC)
        wqB = v_(abf[:, 0:8192], DC)
        wvB = v_(abf[:, 8192:16384], DC)
        qTs = v_(abf[:, 16384:16384 + DC * SB], DC)
        vTs = v_(abf[:, 16384 + DC * SB:16384 + 2 * DC * SB], DC)
        qvs = v_(abf[:, 16384 + 2 * DC * SB:16384 + 3 * DC * SB], DC)
        hA = v_(abf[:, 0:16384], HC)
        hB = v_(abf[:, 16384:32768], HC)
        xpB = v_(abf[:, 0:4096], DC)
        cellB = v_(abf[:, 4096:8192], DC)
        wSlotC = v_(abf[:, 8192:16384], DC)
        wSlotD = v_(abf[:, 16384:24576], DC)
        wSlotE = v_(abf[:, 24576:32768], DC)
        alB = v_(af32[:, 0:4096], DC)
        bvB = v_(af32[:, 4096:8192], DC)
        stg32 = [af32[:, 0:4096], af32[:, 4096:8192]]
        stgbf = [abf[:, 0:4096], abf[:, 4096:8192]]
        z_f = v_(sc_m[:, 0:16], DC)
        zq_m = sc_m[:, 16:32]
        zq_v = v_(zq_m, DC)
        sg = sc_m[:, 32:34]
        sgq = sc_m[:, 34:36]
        rinv = sc_m[:, 36:38]
        z3 = [v_(sc_m[:, 160 + 16 * k:176 + 16 * k], DC) for k in range(3)]
        zqP = [sc_m[:, 208:224], sc_m[:, 224:240]]
        sgP = [sc_m[:, 240:242], sc_m[:, 242:244]]
        sgqP = [sc_m[:, 244:246], sc_m[:, 246:248]]
        rinvP = [sc_m[:, 248:250], sc_m[:, 250:252]]
        lnT = sc_m[:, 252:254]
        ybuf = [v_(sc_m[:, 272:288], DC), v_(sc_m[:, 288:304], DC)]
        c_mhalf = sc_m[:, 92:93]
        Dg = v_(sc_m[:, 256:272], 8)
        cb = [v_(scb[:, 32:160], DC), v_(scb[:, 160:288], DC)]
        carry = v_(sc_m[:, 48:64], DC)
        nrm = sc_m[:, 96:128]
        c_lneps = sc_m[:, 88:89]
        c_rmseps = sc_m[:, 89:90]
        c_invC = sc_m[:, 90:91]
        c_half = sc_m[:, 91:92]
        h0b = v_(scb[:, 0:16], DC)
        idtb = scb[:, 16:24]
        srw = st[:, 1536:2048]
        ln_tmp = v_(st[:, 0:1024], 2)
        ln_m = st[:, 1024:1536]

        ENGS = ("sync", "gpsimd", "tensor", "vector", "scalar")

        def barrier():
            for eng in ENGS:
                def mk(e):
                    def f(x):
                        if e == "sync":
                            x.wait_ge(dma.h, dma.n)
                            x.wait_ge(wst.h, wst.n)
                            x.wait_ge(sgc[0].h, sgc[0].n)
                            x.wait_ge(sgc[1].h, sgc[1].n)
                            x.wait_ge(wbc[0].h, wbc[0].n)
                            x.wait_ge(wbc[1].h, wbc[1].n)
                        if e == "gpsimd":
                            x.wait_ge(dmo.h, dmo.n)
                            x.wait_ge(soc[0].h, soc[0].n)
                            x.wait_ge(soc[1].h, soc[1].n)
                        ph.inc(x.nop())
                    return f
                getattr(blk, eng)(mk(eng))
            target = ph.n
            for eng in ENGS:
                def mk2(e):
                    def f(x):
                        x.wait_ge(ph.h, target)
                    return f
                getattr(blk, eng)(mk2(eng))

        def load_wslot(src_ap, slot_view):
            s0 = [sgc[0].n, sgc[1].n]
            v0 = dve.n
            @blk.sync
            def _(sync):
                for half in range(2):
                    sync.wait_ge(dve.h, v0)
                    sgc[half].inc(sync.dma_start(
                        out=stg32[half].rearrange("p (a j) -> p a j", a=4),
                        in_=src_ap[half * 512:(half + 1) * 512, :]
                            .rearrange("(a p) j -> p a j", p=128)), 16)
            @blk.vector
            def _(v):
                for half in range(2):
                    v.wait_ge(sgc[half].h, s0[half] + 16)
                    dve.inc(v.tensor_copy(
                        slot_view[:, half * 4:(half + 1) * 4, :],
                        stg32[half].rearrange("p (a j) -> p a j", a=4)))

        def emit_ln(x_src, sb0):
            d0, a0, p0, v0 = dma.n, act.n, pe.n, dve.n
            @blk.sync
            def _(sync):
                sync.wait_ge(dmo.h, dmo.n)
                dma.inc(sync.dma_start(
                    out=x_t[:],
                    in_=x_src.rearrange("(a p) t -> p a t", p=128)[:, :, sb0:sb0 + SB]), 16)
            @blk.scalar
            def _(s):
                s.wait_ge(dma.h, d0 + 16)
                for dcc in range(DC):
                    if dcc >= 2:
                        s.wait_ge(pe.h, p0 + DC + dcc - 1)
                    act.inc(s.activation(out=ln_tmp[:, dcc % 2, :], in_=x_t[:, dcc, :],
                                         func=AF.Square))
            @blk.tensor
            def _(t):
                t.wait_ge(dma.h, d0 + 16)
                for dcc in range(DC):
                    ins = t.matmul(psB[:, 0, :], ones_f[:], x_t[:, dcc, :],
                                   start=(dcc == 0), stop=(dcc == DC - 1))
                    pe.inc(ins)
                for dcc in range(DC):
                    t.wait_ge(act.h, a0 + dcc + 1)
                    ins = t.matmul(psB[:, 1, :], ones_f[:], ln_tmp[:, dcc % 2, :],
                                   start=(dcc == 0), stop=(dcc == DC - 1))
                    pe.inc(ins)
            @blk.vector
            def _(v):
                v.wait_ge(pe.h, p0 + 2 * DC)
                v.tensor_scalar(out=ln_m[:], in0=psB[:, 0, :], scalar1=1.0 / C,
                                scalar2=None, op0=ALU.mult)
                v.drain()
                dve.inc(v.tensor_mul(ln_tmp[:, 0, :], ln_m[:], ln_m[:]))
                v.drain()
                dve.inc(v.scalar_tensor_tensor(
                    out=ln_tmp[:, 1, :], in0=psB[:, 1, :], scalar=1.0 / C,
                    in1=ln_tmp[:, 0, :], op0=ALU.mult, op1=ALU.subtract))
            @blk.scalar
            def _(s):
                s.wait_ge(dve.h, v0 + 2)
                act.inc(s.activation(out=ln_tmp[:, 0, :], in_=ln_tmp[:, 1, :],
                                     func=AF.Sqrt, bias=c_lneps))
            @blk.vector
            def _(v):
                v.wait_ge(act.h, a0 + DC + 1)
                dve.inc(v.reciprocal(ln_tmp[:, 1, :], ln_tmp[:, 0, :]))
                v.drain()
                for dcc in range(DC):
                    v.tensor_sub(ln_tmp[:, 0, :], x_t[:, dcc, :], ln_m[:])
                    v.drain()
                    ins = v.tensor_mul(xn_s[:, dcc, :], ln_tmp[:, 0, :], ln_tmp[:, 1, :])
                    v.drain()
                dve.inc(ins)

        def emit_proj8(slot_view, evac_fn, moving_view, wait_dve_ready):
            p0 = pe.n
            @blk.tensor
            def _(t):
                t.wait_ge(dve.h, wait_dve_ready)
                for mt in range(DC):
                    if mt >= 4:
                        t.wait_ge(act.h, evac_fn["a0"] + (mt - 3)) if evac_fn.get("on_act") \
                            else t.wait_ge(dve.h, evac_fn["v0"] + (mt - 3))
                    for kc in range(DC):
                        ins = t.matmul(psA[:, mt % 4, :],
                                       slot_view[:, kc, mt * 128:(mt + 1) * 128],
                                       moving_view[:, kc, :],
                                       start=(kc == 0), stop=(kc == DC - 1))
                    pe.inc(ins)
            return p0

        @blk.sync
        def _(sync):
            for apsrc, col, w in [(s1, 0, 32), (s2, 32, 32), (sh1, 64, 32),
                                  (sh2, 96, 32), (s3, 128, 8), (b_al, 136, 8),
                                  (b_be, 144, 8), (b_v, 152, 8)]:
                dma.inc(sync.dma_start(out=srw[:, col:col + w], in_=apsrc[:, :]), 16)
            dma.inc(sync.dma_start(out=sc_m[:, 40:48], in_=identv[:, :]), 16)
        d_const = dma.n

        @blk.gpsimd
        def _(g):
            gp.inc(g.iota(idmi[:], pattern=[[1, 128]], base=0, channel_multiplier=-1))

        @blk.vector
        def _(v):
            v.memset(ones_f[:], 1.0)
            v.memset(carry[:], 0.0)
            v.memset(c_lneps[:], LN_EPS)
            v.memset(c_rmseps[:], RMS_EPS)
            v.memset(c_invC[:], 1.0 / C)
            v.memset(c_half[:], 0.5)
            v.memset(c_mhalf[:], -0.5)
            v.wait_ge(gp.h, 1)
            v.tensor_single_scalar(out=idm[:], in_=idmi[:], scalar=0, op=ALU.is_equal)
            v.wait_ge(dma.h, d_const)
            v.tensor_copy(idtb[:], sc_m[:, 40:48])
            v.drain()
            for b in range(B):
                dve.inc(v.tensor_copy(h0b[:, :, b], idtb[:]))

        cast_jobs = []
        for src, dst, K, nmt in [(w1T, w1Bd, C, HC), (w2T, w2Bd, C, HC),
                                 (h1T, h1Bd, H, HC), (h2T, h2Bd, H, HC),
                                 (w3T, w3Bd, H, DC), (wcT, wcBd, C, DC),
                                 (winT, winBd, C, DC), (waT, waBd, C, DC),
                                 (wbT, wbBd, C, DC), (wvbT, wvBd, C, DC),
                                 (woT, woBd, C, DC)]:
            for kc in range(K // 128):
                cast_jobs.append((src, dst, nmt, kc))
        v0, o0 = dve.n, dmo.n
        s0c = [sgc[0].n, sgc[1].n]

        @blk.sync
        def _(sync):
            for i, (src, dst, nmt, kc) in enumerate(cast_jobs):
                if i >= 2:
                    sync.wait_ge(dve.h, v0 + (i - 1))
                sgc[i % 2].inc(sync.dma_start(out=stg32[i % 2][:, 0:nmt * 128],
                                              in_=src[kc * 128:(kc + 1) * 128, :]), 16)

        so0 = [soc[0].n, soc[1].n]

        @blk.vector
        def _(v):
            for i, (src, dst, nmt, kc) in enumerate(cast_jobs):
                v.wait_ge(sgc[i % 2].h, s0c[i % 2] + 16 * (i // 2 + 1))
                if i >= 2:
                    v.wait_ge(soc[i % 2].h, so0[i % 2] + 16 * ((i - 2) // 2 + 1))
                dve.inc(v.tensor_copy(stgbf[i % 2][:, 0:nmt * 128],
                                      stg32[i % 2][:, 0:nmt * 128]))

        @blk.gpsimd
        def _(g):
            for i, (src, dst, nmt, kc) in enumerate(cast_jobs):
                g.wait_ge(dve.h, v0 + (i + 1))
                soc[i % 2].inc(g.dma_start(
                    out=dst[:, :, kc, :].transpose([1, 0, 2]),
                    in_=stgbf[i % 2][:, 0:nmt * 128]
                        .rearrange("p (mt m) -> p mt m", m=128)), 16)

        barrier()

        def emit_polar(W_in, out_slot):
            d0 = dma.n
            @blk.sync
            def _(sync):
                dma.inc(sync.dma_start(
                    out=Xf[:], in_=W_in.rearrange("(a p) j -> p a j", p=128)), 16)

            a0, v0, p0 = act.n, dve.n, pe.n
            @blk.scalar
            def _(s):
                s.wait_ge(dma.h, d0 + 16)
                for j in range(2 * DC):
                    dcc, hh = j // 2, j % 2
                    if j >= 4:
                        s.wait_ge(dve.h, v0 + (j - 3))
                    act.inc(s.activation(out=x_t[:, j % 4, :],
                                         in_=Xf[:, dcc, hh * 512:(hh + 1) * 512],
                                         func=AF.Square))
            @blk.vector
            def _(v):
                for j in range(2 * DC):
                    v.wait_ge(act.h, a0 + j + 1)
                    dve.inc(v.tensor_reduce(out=nrm[:, j:j + 1],
                                            in_=x_t[:, j % 4, :],
                                            axis=mybir.AxisListType.X, op=ALU.add))
                v.drain()
                dve.inc(v.tensor_reduce(out=nrm[:, 16:17], in_=nrm[:, 0:16],
                                        axis=mybir.AxisListType.X, op=ALU.add))
            @blk.tensor
            def _(t):
                t.wait_ge(dve.h, v0 + 2 * DC + 1)
                pe.inc(t.matmul(psB[:, 0, 0:1], ones_f[:], nrm[:, 16:17],
                                start=True, stop=True))
            @blk.scalar
            def _(s):
                s.wait_ge(pe.h, p0 + 1)
                act.inc(s.activation(out=nrm[:, 17:18], in_=psB[:, 0, 0:1], func=AF.Sqrt))
            @blk.vector
            def _(v):
                v.wait_ge(act.h, a0 + 2 * DC + 1)
                v.tensor_scalar(out=nrm[:, 18:19], in0=nrm[:, 17:18],
                                scalar1=1e-8, scalar2=None, op0=ALU.add)
                v.drain()
                v.reciprocal(nrm[:, 19:20], nrm[:, 18:19])
                v.drain()
                for dcc in range(DC):
                    ins = v.tensor_scalar(out=Xf[:, dcc, :], in0=Xf[:, dcc, :],
                                          scalar1=nrm[:, 19:20], scalar2=None,
                                          op0=ALU.mult)
                dve.inc(ins)

            for it in range(NS_ITERS + 1):
                last = (it == NS_ITERS)
                vx = dve.n
                @blk.vector
                def _(v):
                    v.wait_ge(dve.h, vx)
                    v.drain()
                    for dcc in range(DC):
                        ins = v.tensor_copy(Xbf[:, dcc, :], Xf[:, dcc, :])
                    dve.inc(ins)
                vcast = dve.n
                p0t, v0t = pe.n, dve.n
                @blk.tensor
                def _(t):
                    t.wait_ge(dve.h, vcast)
                    k = 0
                    for ic in range(DC):
                        for jc in range(DC):
                            if k >= 2:
                                t.wait_ge(dve.h, v0t + (k - 1))
                            pe.inc(t.transpose(out=psT[:, k % 2, 0:128],
                                               in_=Xbf[:, ic, jc * 128:(jc + 1) * 128],
                                               identity=idm[:]))
                            k += 1
                @blk.vector
                def _(v):
                    k = 0
                    for ic in range(DC):
                        for jc in range(DC):
                            v.wait_ge(pe.h, p0t + k + 1)
                            dst = out_slot if last else XTb
                            dve.inc(v.tensor_copy(dst[:, jc, ic * 128:(ic + 1) * 128],
                                                  psT[:, k % 2, 0:128]))
                            k += 1
                if last:
                    break
                vtrans = dve.n
                p0g, a0g = pe.n, act.n
                @blk.tensor
                def _(t):
                    t.wait_ge(dve.h, vtrans)
                    k = 0
                    for mj in range(DC):
                        for nh in range(2):
                            if k >= 4:
                                t.wait_ge(act.h, a0g + (k - 3))
                            for kc in range(DC):
                                ins = t.matmul(psA[:, k % 4, :],
                                               Xbf[:, kc, mj * 128:(mj + 1) * 128],
                                               Xbf[:, kc, nh * 512:(nh + 1) * 512],
                                               start=(kc == 0), stop=(kc == DC - 1))
                            pe.inc(ins)
                            k += 1
                @blk.scalar
                def _(s):
                    k = 0
                    for mj in range(DC):
                        for nh in range(2):
                            s.wait_ge(pe.h, p0g + k + 1)
                            act.inc(s.activation(
                                out=Gb[:, mj, nh * 512:(nh + 1) * 512],
                                in_=psA[:, k % 4, :], func=AF.Copy, scale=c_half))
                            k += 1
                p0p, v0p = pe.n, dve.n
                @blk.tensor
                def _(t):
                    t.wait_ge(act.h, a0g + 16)
                    k = 0
                    for mi in range(DC):
                        for nh in range(2):
                            if k >= 4:
                                t.wait_ge(dve.h, v0p + (k - 3))
                            for kj in range(DC):
                                ins = t.matmul(psA[:, k % 4, :],
                                               XTb[:, kj, mi * 128:(mi + 1) * 128],
                                               Gb[:, kj, nh * 512:(nh + 1) * 512],
                                               start=(kj == 0), stop=(kj == DC - 1))
                            pe.inc(ins)
                            k += 1
                @blk.vector
                def _(v):
                    k = 0
                    for mi in range(DC):
                        for nh in range(2):
                            v.wait_ge(pe.h, p0p + k + 1)
                            dve.inc(v.scalar_tensor_tensor(
                                out=Xf[:, mi, nh * 512:(nh + 1) * 512],
                                in0=Xf[:, mi, nh * 512:(nh + 1) * 512],
                                scalar=1.5, in1=psA[:, k % 4, :],
                                op0=ALU.mult, op1=ALU.subtract))
                            k += 1

        emit_polar(w_lh, wslotA)
        barrier()
        emit_polar(w_rh, wslotB)
        barrier()

        def dump_dbg_slot(slot_view, dst_ap):
            if not debug:
                return
            v0d = dve.n
            @blk.vector
            def _(v):
                v.wait_ge(dmo.h, dmo.n)
                for dcc in range(DC):
                    ins = v.tensor_copy(Xf[:, dcc, :], slot_view[:, dcc, :])
                dve.inc(ins)
            @blk.gpsimd
            def _(g):
                g.wait_ge(dve.h, v0d + 1)
                dmo.inc(g.dma_start(out=dst_ap.rearrange("(a p) j -> p a j", p=128),
                                    in_=Xf[:]), 16)
            barrier()

        dump_dbg_slot(wslotA, dbg.get("d_olh"))
        dump_dbg_slot(wslotB, dbg.get("d_orh"))

        load_wslot(wqT, wqB)
        load_wslot(wvT, wvB)
        barrier()
        for sb in range(NSB):
            sb0 = sb * SB
            emit_ln(xT, sb0)
            vln = dve.n
            p0q, v0q = pe.n, dve.n
            @blk.tensor
            def _(t):
                t.wait_ge(dve.h, vln)
                k = 0
                for slotw in (wqB, wvB):
                    for mt in range(DC):
                        if k >= 4:
                            t.wait_ge(dve.h, v0q + (k - 3))
                        for kc in range(DC):
                            ins = t.matmul(psA[:, k % 4, :],
                                           slotw[:, kc, mt * 128:(mt + 1) * 128],
                                           xn_s[:, kc, :],
                                           start=(kc == 0), stop=(kc == DC - 1))
                        pe.inc(ins)
                        k += 1
            @blk.vector
            def _(v):
                k = 0
                for dst in (qTs, vTs):
                    for mt in range(DC):
                        v.wait_ge(pe.h, p0q + k + 1)
                        dve.inc(v.tensor_copy(dst[:, mt, :], psA[:, k % 4, :]))
                        k += 1
            vqv = dve.n
            p0r, v0r = pe.n, dve.n
            @blk.tensor
            def _(t):
                t.wait_ge(dve.h, vqv)
                for mt in range(DC):
                    if mt >= 4:
                        t.wait_ge(dve.h, v0r + (mt - 3))
                    for kc in range(DC):
                        ins = t.matmul(psA[:, mt % 4, :],
                                       wslotB[:, kc, mt * 128:(mt + 1) * 128],
                                       qTs[:, kc, :],
                                       start=(kc == 0), stop=(kc == DC - 1))
                    pe.inc(ins)
            @blk.vector
            def _(v):
                for mt in range(DC):
                    v.wait_ge(pe.h, p0r + mt + 1)
                    dve.inc(v.tensor_copy(qvs[:, mt, :], psA[:, mt % 4, :]))
            vst = dve.n
            @blk.gpsimd
            def _(g):
                g.wait_ge(dve.h, vst)
                dmo.inc(g.dma_start(
                    out=vD.rearrange("(a p) t -> p a t", p=128)[:, :, sb0:sb0 + SB],
                    in_=vTs[:]), 16)
                dmo.inc(g.dma_start(
                    out=qrD.rearrange("(a p) t -> p a t", p=128)[:, :, sb0:sb0 + SB],
                    in_=qvs[:]), 16)
        barrier()
        if debug:
            @blk.gpsimd
            def _(g):
                g.wait_ge(dmo.h, dmo.n)
                dmo.inc(g.dma_start(out=dbg["d_qr"], in_=qrD[:, :]), 16)
            barrier()

        RB = 8
        @blk.vector
        def _(v):
            v.memset(cb[0][:], 0.0)
            v.memset(cb[1][:], 0.0)
            dve.inc(v.memset(Dg[:], 1.0))
            v.drain()
        @blk.scalar
        def _(s):
            s.wait_ge(dve.h, dve.n)
            act.inc(s.activation(out=cb[0][:, :, 0:2], in_=h0b[:], func=AF.Copy))
        pre_seed_act = act.n

        for ch in range(NCH):
            c0 = ch * 2 * SCH
            d0s = dma.n
            @blk.sync
            def _(sync):
                sync.wait_ge(dve.h, dve.n)
                sync.wait_ge(act.h, act.n)
                dma.inc(sync.dma_start(
                    out=qr_s[:],
                    in_=qrD.rearrange("(a p) t -> p a t", p=128)[:, :, c0:c0 + 2 * SCH]), 16)
            o_pr = dmo.n
            p0s, v0s, a0s = pe.n, dve.n, act.n

            stats_m = {}
            passA_m = {}
            passB_m = {}
            n = p0s
            for tt in range(SCH):
                if tt >= 1:
                    n += 1; stats_m[tt - 1] = n
                n += 1; passA_m[tt] = n
                n += 1; passB_m[tt] = n
            n += 1; stats_m[SCH - 1] = n
            pe_end = n

            evac_m = {}
            exp_m = {}
            n = a0s
            n += 1
            for tt in range(SCH):
                if tt >= 1:
                    n += 1; exp_m[tt - 1] = n
                n += 1; evac_m[tt] = n
            n += 1; exp_m[SCH - 1] = n
            act_end = n

            partB_m = {}
            zsq_m = {}
            apply_m = {}
            n = v0s
            for tt in range(SCH):
                if tt >= 1:
                    n += 1; partB_m[tt - 1] = n
                if tt >= 1 and tt % RB == 0:
                    n += 1; apply_m[tt - 1] = n
                n += 1; zsq_m[tt] = n
                if tt >= 1 and tt % RB != 0:
                    n += 1; apply_m[tt - 1] = n
            n += 1; partB_m[SCH - 1] = n
            n += 1; apply_m[SCH - 1] = n
            dve_end = n

            @blk.tensor
            def _(t):
                for tt in range(SCH):
                    if tt >= 1:
                        t.wait_ge(dve.h, zsq_m[tt - 1])
                        pe.inc(t.matmul(psB[:, (tt - 1) % 2, 0:16], ones_f[:],
                                        zqP[(tt - 1) % 2], start=True, stop=True))
                    if tt == 0:
                        t.wait_ge(dma.h, d0s + 16)
                        t.wait_ge(act.h, a0s + 1)
                        if ch == 0:
                            t.wait_ge(act.h, pre_seed_act + 1)
                        t.wait_ge(dve.h, v0s)
                    else:
                        t.wait_ge(act.h, evac_m[tt - 1])
                        if tt % RB == 0:
                            t.wait_ge(dve.h, apply_m[tt - 1])
                        if tt >= 3:
                            t.wait_ge(dve.h, zsq_m[tt - 3])
                    zv = v_(psA[:, tt % 4, 0:128], DC)
                    for dcc in range(DC):
                        for kc in range(DC):
                            ins = t.matmul(zv[:, dcc, :],
                                           wslotA[:, kc, dcc * 128:(dcc + 1) * 128],
                                           cb[tt % 2][:, kc, :],
                                           start=(kc == 0), stop=(kc == DC - 1))
                        if dcc == 3:
                            pe.inc(ins)
                    pe.inc(ins)
                t.wait_ge(dve.h, zsq_m[SCH - 1])
                pe.inc(t.matmul(psB[:, (SCH - 1) % 2, 0:16], ones_f[:],
                                zqP[(SCH - 1) % 2], start=True, stop=True))

            @blk.scalar
            def _(s):
                s.wait_ge(dma.h, d0s + 16)
                s.wait_ge(dve.h, v0s)
                act.inc(s.activation(out=cb[0][:, :, 2:4],
                                     in_=qr_s[:, :, 0:2], func=AF.Copy))
                for tt in range(SCH):
                    j = tt % RB
                    zv = v_(psA[:, tt % 4, 0:128], DC)
                    if tt >= 1:
                        s.wait_ge(dve.h, partB_m[tt - 1])
                        s.activation(out=lnT[:], in_=sgP[(tt - 1) % 2], func=AF.Ln,
                                     scale=c_invC, bias=c_rmseps)
                        s.drain()
                        act.inc(s.activation(out=rinvP[(tt - 1) % 2], in_=lnT[:],
                                             func=AF.Exp, scale=c_mhalf))
                    s.wait_ge(pe.h, passB_m[tt])
                    ins = s.activation(out=cb[(tt + 1) % 2][:, :, :], in_=zv[:, :, :],
                                       func=AF.Copy)
                    if j <= 5 or (j == 7 and tt != SCH - 1):
                        s.drain()
                    if j <= 5:
                        si = j + 1
                        col = 2 * (tt - j + si)
                        ins = s.activation(out=cb[(tt + 1) % 2][:, :, 2 + 2 * si:4 + 2 * si],
                                           in_=qr_s[:, :, col:col + 2], func=AF.Copy)
                    elif j == 7 and tt != SCH - 1:
                        col = 2 * (tt + 1)
                        ins = s.activation(out=cb[(tt + 1) % 2][:, :, 2:4],
                                           in_=qr_s[:, :, col:col + 2], func=AF.Copy)
                    act.inc(ins)
                s.wait_ge(dve.h, partB_m[SCH - 1])
                s.activation(out=lnT[:], in_=sgP[(SCH - 1) % 2], func=AF.Ln,
                             scale=c_invC, bias=c_rmseps)
                s.drain()
                act.inc(s.activation(out=rinvP[(SCH - 1) % 2], in_=lnT[:],
                                     func=AF.Exp, scale=c_mhalf))

            def emit_apply(v, ts):
                j2 = ts % RB
                v.wait_ge(act.h, exp_m[ts])
                v.drain()
                for b in range(B):
                    v.tensor_scalar(out=qs_s[:, :, 2 * ts + b],
                                    in0=z3[ts % 3][:, :, b],
                                    scalar1=rinvP[ts % 2][:, b:b + 1],
                                    scalar2=None, op0=ALU.mult)
                if j2 == RB - 1:
                    for b in range(B):
                        ins = v.tensor_scalar(out=cb[(ts + 1) % 2][:, :, b],
                                              in0=z3[ts % 3][:, :, b],
                                              scalar1=rinvP[ts % 2][:, b:b + 1],
                                              scalar2=None, op0=ALU.mult)
                else:
                    rb_ = rinvP[ts % 2].rearrange("p (a b) -> p a b", a=1).broadcast_to([128, 8, 2])
                    v.tensor_tensor(Dg[:], Dg[:], rb_, ALU.mult)
                    v.drain()
                    ins = v.memset(Dg[:, j2 + 1:j2 + 2, :], 1.0)
                dve.inc(ins)

            @blk.vector
            def _(v):
                for tt in range(SCH):
                    j = tt % RB
                    zv = v_(psA[:, tt % 4, 0:128], DC)
                    zt = z3[tt % 3]
                    if tt >= 1:
                        v.wait_ge(pe.h, stats_m[tt - 1])
                        dve.inc(v.tensor_reduce(
                            out=sgP[(tt - 1) % 2],
                            in_=v_(psB[:, (tt - 1) % 2, 0:16], DC).transpose([0, 2, 1]),
                            axis=mybir.AxisListType.X, op=ALU.add))
                    if j == 0:
                        if tt >= 1:
                            emit_apply(v, tt - 1)
                        v.drain()
                        v.memset(Dg[:, 7:8, :], 1.0)
                        v.memset(Dg[:, 0:1, :], 1.0)
                    if j >= 1:
                        v.wait_ge(pe.h, passB_m[tt])
                        v.wait_ge(act.h, evac_m[tt])
                        yb = ybuf[tt % 2]
                        zvp = v_(psA[:, (tt - 1) % 4, 0:128], DC)
                        gamb = Dg[:, 7:8, :].broadcast_to([128, 8, 2])
                        v.drain()
                        v.tensor_tensor(yb[:], zv[:, :, 0:2], gamb, ALU.mult)
                        for i in range(j):
                            v.drain()
                            for b in range(B):
                                v.scalar_tensor_tensor(
                                    out=yb[:, :, b], in0=zvp[:, :, 2 + 2 * i + b],
                                    scalar=Dg[:, i, b:b + 1], in1=yb[:, :, b],
                                    op0=ALU.mult, op1=ALU.add)
                        v.drain()
                    else:
                        v.wait_ge(pe.h, passB_m[tt])
                        v.wait_ge(act.h, evac_m[tt])
                    if tt == 0 and ch > 0:
                        v.wait_ge(dmo.h, o_pr)
                    if j == 0:
                        v.tensor_add(zt[:], zv[:, :, 0:2], qr_s[:, :, 2 * tt:2 * tt + 2])
                        v.drain()
                    else:
                        v.wait_ge(act.h, exp_m[tt - 1])
                        for b in range(B):
                            v.scalar_tensor_tensor(
                                out=zt[:, :, b], in0=ybuf[tt % 2][:, :, b],
                                scalar=rinvP[(tt - 1) % 2][:, b:b + 1],
                                in1=qr_s[:, :, 2 * tt + b:2 * tt + b + 1].squeeze(2)
                                if False else qr_s[:, :, 2 * tt + b],
                                op0=ALU.mult, op1=ALU.add)
                        v.drain()
                    dve.inc(v.tensor_mul(v_(zqP[tt % 2], DC)[:], zt[:], zt[:]))
                    if tt >= 1 and tt % RB != 0:
                        emit_apply(v, tt - 1)
                v.wait_ge(pe.h, stats_m[SCH - 1])
                dve.inc(v.tensor_reduce(
                    out=sgP[(SCH - 1) % 2],
                    in_=v_(psB[:, (SCH - 1) % 2, 0:16], DC).transpose([0, 2, 1]),
                    axis=mybir.AxisListType.X, op=ALU.add))
                v.wait_ge(act.h, exp_m[SCH - 1])
                ts = SCH - 1
                for b in range(B):
                    v.tensor_scalar(out=qs_s[:, :, 2 * ts + b], in0=z3[ts % 3][:, :, b],
                                    scalar1=rinvP[ts % 2][:, b:b + 1], scalar2=None,
                                    op0=ALU.mult)
                for b in range(B):
                    ins = v.tensor_scalar(out=cb[0][:, :, b], in0=z3[ts % 3][:, :, b],
                                          scalar1=rinvP[ts % 2][:, b:b + 1], scalar2=None,
                                          op0=ALU.mult)
                dve.inc(ins)

            @blk.gpsimd
            def _(g):
                g.wait_ge(dve.h, dve_end)
                dmo.inc(g.dma_start(
                    out=qsD.rearrange("(a p) t -> p a t", p=128)[:, :, c0:c0 + 2 * SCH],
                    in_=qs_s[:]), 16)
        barrier()
        def emit_p4_sb(sb):
            sb0 = sb * SB
            d0y, v0y, p0y = dma.n, dve.n, pe.n
            wb0y = [wbc[0].n, wbc[1].n]
            pe_snap = pe.n
            @blk.sync
            def _(sync):
                sync.wait_ge(dmo.h, dmo.n)
                dma.inc(sync.dma_start(
                    out=x_t[:],
                    in_=xT.rearrange("(a p) t -> p a t", p=128)[:, :, sb0:sb0 + SB]), 16)
                dma.inc(sync.dma_start(out=qTs[:], in_=qs_src(sb)), 16)
                dma.inc(sync.dma_start(
                    out=vTs[:],
                    in_=vD.rearrange("(a p) t -> p a t", p=128)[:, :, sb0:sb0 + SB]), 16)
                sync.wait_ge(pe.h, pe_snap)
                for mt in range(DC):
                    if mt >= 2:
                        sync.wait_ge(pe.h, p0y + (mt - 1))
                    wbc[mt % 2].inc(sync.dma_start(
                        out=wbuf[:, mt % 2, 0:DC, :], in_=wcBd[mt]), 16)
            @blk.vector
            def _(v):
                v.wait_ge(dma.h, d0y + 48)
                dve.inc(v.tensor_mul(qvs[:], qTs[:], vTs[:]))
                for mt in range(DC):
                    v.wait_ge(pe.h, p0y + mt + 1)
                    dve.inc(v.tensor_add(x_t[:, mt, :], psA[:, mt % 4, :], x_t[:, mt, :]))
            @blk.tensor
            def _(t):
                t.wait_ge(dve.h, v0y + 1)
                for mt in range(DC):
                    t.wait_ge(wbc[mt % 2].h, wb0y[mt % 2] + 16 * (mt // 2 + 1))
                    if mt >= 4:
                        t.wait_ge(dve.h, v0y + 1 + (mt - 3))
                    for kc in range(DC):
                        ins = t.matmul(psA[:, mt % 4, :],
                                       wbuf[:, mt % 2, kc, :],
                                       qvs[:, kc, :],
                                       start=(kc == 0), stop=(kc == DC - 1))
                    pe.inc(ins)
            vy = dve.n
            @blk.gpsimd
            def _(g):
                g.wait_ge(dve.h, vy)
                dmo.inc(g.dma_start(
                    out=x_res.rearrange("(a p) t -> p a t", p=128)[:, :, sb0:sb0 + SB],
                    in_=x_t[:]), 16)

        def emit_ffn_sb(sb, x_src, x_dst):
            if True:
                sb0 = sb * SB
                emit_ln(x_src, sb0)
                vln = dve.n
                w0, p0, a0, v0f = wst.n, pe.n, act.n, dve.n
                pe_snap = pe.n
                wb0 = [wbc[0].n, wbc[1].n]
                @blk.sync
                def _(sync):
                    sync.wait_ge(pe.h, pe_snap)
                    for mt in range(HC):
                        if mt >= 2:
                            sync.wait_ge(pe.h, p0 + (mt - 1))
                        wbc[mt % 2].inc(sync.dma_start(
                            out=wbuf[:, mt % 2, 0:DC, :], in_=w1Bd[mt]), 16)
                        wbc[mt % 2].inc(sync.dma_start(
                            out=wbuf[:, mt % 2, DC:2 * DC, :], in_=w2Bd[mt]), 16)
                @blk.tensor
                def _(t):
                    t.wait_ge(dve.h, vln)
                    for mt in range(HC):
                        t.wait_ge(wbc[mt % 2].h, wb0[mt % 2] + 32 * (mt // 2 + 1))
                        if mt >= 2:
                            t.wait_ge(dve.h, v0f + (mt - 1))
                        for kc in range(DC):
                            t.matmul(psA[:, 2 * (mt % 2), :],
                                     wbuf[:, mt % 2, kc, :], xn_s[:, kc, :],
                                     start=(kc == 0), stop=(kc == DC - 1))
                        for kc in range(DC):
                            ins = t.matmul(psA[:, 2 * (mt % 2) + 1, :],
                                           wbuf[:, mt % 2, DC + kc, :], xn_s[:, kc, :],
                                           start=(kc == 0), stop=(kc == DC - 1))
                        pe.inc(ins)
                @blk.scalar
                def _(s):
                    for mt in range(HC):
                        s.wait_ge(pe.h, p0 + mt + 1)
                        if mt >= 2:
                            s.wait_ge(dve.h, v0f + (mt - 1))
                        act.inc(s.activation(out=ln_tmp[:, mt % 2, :],
                                             in_=psA[:, 2 * (mt % 2), :], func=AF.Silu,
                                             scale=srw[:, mt:mt + 1]))
                @blk.vector
                def _(v):
                    for mt in range(HC):
                        v.wait_ge(act.h, a0 + mt + 1)
                        dve.inc(v.scalar_tensor_tensor(
                            out=hA[:, mt, :], in0=psA[:, 2 * (mt % 2) + 1, :],
                            scalar=srw[:, 32 + mt:32 + mt + 1],
                            in1=ln_tmp[:, mt % 2, :], op0=ALU.mult, op1=ALU.mult))
                vw12 = dve.n

                p1h, a1h = pe.n, act.n
                wb1 = [wbc[0].n, wbc[1].n]
                @blk.sync
                def _(sync):
                    sync.wait_ge(pe.h, p1h)
                    for mt in range(HC):
                        if mt >= 2:
                            sync.wait_ge(pe.h, p1h + (mt - 1))
                        wbc[mt % 2].inc(sync.dma_start(out=wbuf[:, mt % 2, :, :], in_=h1Bd[mt]), 16)
                @blk.tensor
                def _(t):
                    t.wait_ge(dve.h, vw12)
                    for mt in range(HC):
                        t.wait_ge(wbc[mt % 2].h, wb1[mt % 2] + 16 * (mt // 2 + 1))
                        if mt >= 4:
                            t.wait_ge(act.h, a1h + (mt - 3))
                        for kc in range(HC):
                            ins = t.matmul(psA[:, mt % 4, :],
                                           wbuf[:, mt % 2, kc, :], hA[:, kc, :],
                                           start=(kc == 0), stop=(kc == HC - 1))
                        pe.inc(ins)
                @blk.scalar
                def _(s):
                    for mt in range(HC):
                        s.wait_ge(pe.h, p1h + mt + 1)
                        act.inc(s.activation(out=hB[:, mt, :], in_=psA[:, mt % 4, :],
                                             func=AF.Gelu,
                                             scale=srw[:, 64 + mt:64 + mt + 1]))
                a_h1 = act.n

                p2h, a2h = pe.n, act.n
                wb2 = [wbc[0].n, wbc[1].n]
                @blk.sync
                def _(sync):
                    sync.wait_ge(pe.h, p2h)
                    for mt in range(HC):
                        if mt >= 2:
                            sync.wait_ge(pe.h, p2h + (mt - 1))
                        wbc[mt % 2].inc(sync.dma_start(out=wbuf[:, mt % 2, :, :], in_=h2Bd[mt]), 16)
                @blk.tensor
                def _(t):
                    t.wait_ge(act.h, a_h1)
                    for mt in range(HC):
                        t.wait_ge(wbc[mt % 2].h, wb2[mt % 2] + 16 * (mt // 2 + 1))
                        if mt >= 4:
                            t.wait_ge(act.h, a2h + (mt - 3))
                        for kc in range(HC):
                            ins = t.matmul(psA[:, mt % 4, :],
                                           wbuf[:, mt % 2, kc, :], hB[:, kc, :],
                                           start=(kc == 0), stop=(kc == HC - 1))
                        pe.inc(ins)
                @blk.scalar
                def _(s):
                    for mt in range(HC):
                        s.wait_ge(pe.h, p2h + mt + 1)
                        act.inc(s.activation(out=hA[:, mt, :], in_=psA[:, mt % 4, :],
                                             func=AF.Gelu,
                                             scale=srw[:, 96 + mt:96 + mt + 1]))
                a_h2 = act.n

                p3h, v3h = pe.n, dve.n
                wb3 = [wbc[0].n, wbc[1].n]
                @blk.sync
                def _(sync):
                    sync.wait_ge(pe.h, p3h)
                    for mt in range(DC):
                        if mt >= 2:
                            sync.wait_ge(pe.h, p3h + (mt - 1))
                        wbc[mt % 2].inc(sync.dma_start(out=wbuf[:, mt % 2, :, :], in_=w3Bd[mt]), 16)
                @blk.tensor
                def _(t):
                    t.wait_ge(act.h, a_h2)
                    for mt in range(DC):
                        t.wait_ge(wbc[mt % 2].h, wb3[mt % 2] + 16 * (mt // 2 + 1))
                        if mt >= 2:
                            t.wait_ge(dve.h, v3h + (mt - 1))
                        for kc in range(HC):
                            ins = t.matmul(psB[:, mt % 2, :],
                                           wbuf[:, mt % 2, kc, :], hA[:, kc, :],
                                           start=(kc == 0), stop=(kc == HC - 1))
                        pe.inc(ins)
                @blk.vector
                def _(v):
                    for mt in range(DC):
                        v.wait_ge(pe.h, p3h + mt + 1)
                        dve.inc(v.scalar_tensor_tensor(
                            out=x_t[:, mt, :], in0=psB[:, mt % 2, :],
                            scalar=srw[:, 128 + mt:128 + mt + 1],
                            in1=x_t[:, mt, :], op0=ALU.mult, op1=ALU.add))
                vres = dve.n
                @blk.gpsimd
                def _(g):
                    g.wait_ge(dve.h, vres)
                    dmo.inc(g.dma_start(
                        out=x_dst.rearrange("(a p) t -> p a t", p=128)[:, :, sb0:sb0 + SB],
                        in_=x_t[:]), 16)

        def emit_attnb_sb(sb):
            sb0 = sb * SB
            emit_ln(x_res, sb0)
            vln = dve.n
            p0x, a0x = pe.n, act.n
            wb0 = [wbc[0].n, wbc[1].n]
            pe_snap = pe.n
            @blk.sync
            def _(sync):
                sync.wait_ge(pe.h, pe_snap)
                for mt in range(DC):
                    if mt >= 2:
                        sync.wait_ge(pe.h, p0x + (mt - 1))
                    wbc[mt % 2].inc(sync.dma_start(
                        out=wbuf[:, mt % 2, 0:DC, :], in_=winBd[mt]), 16)
            @blk.tensor
            def _(t):
                t.wait_ge(dve.h, vln)
                for mt in range(DC):
                    t.wait_ge(wbc[mt % 2].h, wb0[mt % 2] + 16 * (mt // 2 + 1))
                    if mt >= 4:
                        t.wait_ge(act.h, a0x + (mt - 3))
                    for kc in range(DC):
                        ins = t.matmul(psA[:, mt % 4, :],
                                       wbuf[:, mt % 2, kc, :],
                                       xn_s[:, kc, :],
                                       start=(kc == 0), stop=(kc == DC - 1))
                    pe.inc(ins)
            @blk.scalar
            def _(s):
                for mt in range(DC):
                    s.wait_ge(pe.h, p0x + mt + 1)
                    act.inc(s.activation(out=xpB[:, mt, :], in_=psA[:, mt % 4, :],
                                         func=AF.Silu))
            a_xp = act.n
            p0a, a0a = pe.n, act.n
            wb1 = [wbc[0].n, wbc[1].n]
            pe_snap2 = pe.n
            @blk.sync
            def _(sync):
                sync.wait_ge(pe.h, pe_snap2)
                for mt in range(DC):
                    if mt >= 2:
                        sync.wait_ge(pe.h, p0a + (mt - 1))
                    wbc[mt % 2].inc(sync.dma_start(
                        out=wbuf[:, mt % 2, 0:DC, :], in_=waBd[mt]), 16)
            @blk.tensor
            def _(t):
                t.wait_ge(act.h, a_xp)
                for mt in range(DC):
                    t.wait_ge(wbc[mt % 2].h, wb1[mt % 2] + 16 * (mt // 2 + 1))
                    if mt >= 4:
                        t.wait_ge(act.h, a0a + (mt - 3))
                    for kc in range(DC):
                        ins = t.matmul(psA[:, mt % 4, :],
                                       wbuf[:, mt % 2, kc, :],
                                       xpB[:, kc, :],
                                       start=(kc == 0), stop=(kc == DC - 1))
                    pe.inc(ins)
            @blk.scalar
            def _(s):
                for mt in range(DC):
                    s.wait_ge(pe.h, p0a + mt + 1)
                    act.inc(s.activation(out=alB[:, mt, :], in_=psA[:, mt % 4, :],
                                         func=AF.Sigmoid,
                                         bias=srw[:, 136 + mt:136 + mt + 1]))
            p0b, a0b, v0b = pe.n, act.n, dve.n
            wb2 = [wbc[0].n, wbc[1].n]
            pe_snap3 = pe.n
            @blk.sync
            def _(sync):
                sync.wait_ge(pe.h, pe_snap3)
                for mt in range(DC):
                    if mt >= 2:
                        sync.wait_ge(pe.h, p0b + (mt - 1))
                    wbc[mt % 2].inc(sync.dma_start(
                        out=wbuf[:, mt % 2, 0:DC, :], in_=wbBd[mt]), 16)
                    wbc[mt % 2].inc(sync.dma_start(
                        out=wbuf[:, mt % 2, DC:2 * DC, :], in_=wvBd[mt]), 16)
            @blk.tensor
            def _(t):
                for mt in range(DC):
                    t.wait_ge(wbc[mt % 2].h, wb2[mt % 2] + 32 * (mt // 2 + 1))
                    if mt >= 1:
                        t.wait_ge(dve.h, v0b + mt)
                    for kc in range(DC):
                        t.matmul(psB[:, 0, :],
                                 wbuf[:, mt % 2, kc, :],
                                 xpB[:, kc, :],
                                 start=(kc == 0), stop=(kc == DC - 1))
                    for kc in range(DC):
                        ins = t.matmul(psB[:, 1, :],
                                       wbuf[:, mt % 2, DC + kc, :],
                                       xpB[:, kc, :],
                                       start=(kc == 0), stop=(kc == DC - 1))
                    pe.inc(ins)
            @blk.scalar
            def _(s):
                for mt in range(DC):
                    s.wait_ge(pe.h, p0b + mt + 1)
                    if mt >= 1:
                        s.wait_ge(dve.h, v0b + mt)
                    s.activation(out=ln_tmp[:, 0, :], in_=psB[:, 0, :],
                                 func=AF.Sigmoid, bias=srw[:, 144 + mt:144 + mt + 1])
                    act.inc(s.activation(out=ln_tmp[:, 1, :], in_=psB[:, 1, :],
                                         func=AF.Tanh,
                                         bias=srw[:, 152 + mt:152 + mt + 1]))
            @blk.vector
            def _(v):
                for mt in range(DC):
                    v.wait_ge(act.h, a0b + mt + 1)
                    dve.inc(v.tensor_mul(bvB[:, mt, :], ln_tmp[:, 0, :], ln_tmp[:, 1, :]))
            v0sc = dve.n
            @blk.vector
            def _(v):
                v.wait_ge(dve.h, v0sc)
                v.drain()
                for dcc in range(DC):
                    al_tb = alB[:, dcc, :].rearrange("p (t b) -> p b t", b=B)
                    bv_tb = bvB[:, dcc, :].rearrange("p (t b) -> p b t", b=B)
                    for b in range(B):
                        v.tensor_tensor_scan(
                            out=al_tb[:, b, :], data0=al_tb[:, b, :],
                            data1=bv_tb[:, b, :],
                            initial=carry[:, dcc, b:b + 1],
                            op0=ALU.mult, op1=ALU.add)
                        v.drain()
                        ins = v.tensor_copy(carry[:, dcc, b:b + 1],
                                            al_tb[:, b, SB // B - 1:SB // B])
                dve.inc(ins)
            v_scan = dve.n
            a0c, v0c = act.n, dve.n
            @blk.scalar
            def _(s):
                s.wait_ge(dve.h, v_scan)
                for dcc in range(DC):
                    if dcc >= 2:
                        s.wait_ge(dve.h, v0c + (dcc - 1))
                    act.inc(s.activation(out=ln_tmp[:, dcc % 2, :],
                                         in_=alB[:, dcc, :], func=AF.Silu))
            @blk.vector
            def _(v):
                v.drain()
                for dcc in range(DC):
                    v.wait_ge(act.h, a0c + dcc + 1)
                    dve.inc(v.tensor_mul(cellB[:, dcc, :], alB[:, dcc, :],
                                         ln_tmp[:, dcc % 2, :]))
            v_cell = dve.n
            p0o, v0o = pe.n, dve.n
            wb3 = [wbc[0].n, wbc[1].n]
            pe_snap4 = pe.n
            @blk.sync
            def _(sync):
                sync.wait_ge(pe.h, pe_snap4)
                for mt in range(DC):
                    if mt >= 2:
                        sync.wait_ge(pe.h, p0o + (mt - 1))
                    wbc[mt % 2].inc(sync.dma_start(
                        out=wbuf[:, mt % 2, 0:DC, :], in_=woBd[mt]), 16)
            @blk.tensor
            def _(t):
                t.wait_ge(dve.h, v_cell)
                for mt in range(DC):
                    t.wait_ge(wbc[mt % 2].h, wb3[mt % 2] + 16 * (mt // 2 + 1))
                    if mt >= 4:
                        t.wait_ge(dve.h, v0o + (mt - 3))
                    for kc in range(DC):
                        ins = t.matmul(psA[:, mt % 4, :],
                                       wbuf[:, mt % 2, kc, :],
                                       cellB[:, kc, :],
                                       start=(kc == 0), stop=(kc == DC - 1))
                    pe.inc(ins)
            @blk.vector
            def _(v):
                for mt in range(DC):
                    v.wait_ge(pe.h, p0o + mt + 1)
                    dve.inc(v.tensor_add(x_t[:, mt, :], psA[:, mt % 4, :], x_t[:, mt, :]))
            v_ob = dve.n
            @blk.gpsimd
            def _(g):
                g.wait_ge(dve.h, v_ob)
                dmo.inc(g.dma_start(
                    out=x_res.rearrange("(a p) t -> p a t", p=128)[:, :, sb0:sb0 + SB],
                    in_=x_t[:]), 16)

        def qs_src(sb):
            sb0 = sb * SB
            return qsD.rearrange("(a p) t -> p a t", p=128)[:, :, sb0:sb0 + SB]

        for sb in range(NSB):
            emit_p4_sb(sb)
            emit_ffn_sb(sb, x_res, x_res)
            emit_attnb_sb(sb)
            emit_ffn_sb(sb, x_res, outT)
            barrier()
        barrier()

    return nc


_PROG_CACHE = {}


def _get_prog(debug=False):
    key = debug
    if key not in _PROG_CACHE:
        _PROG_CACHE[key] = build_program(debug=debug)
    return _PROG_CACHE[key]


def kernel(**inputs):
    x = np.asarray(inputs["L0_attn_a_x"], dtype=np.float32)
    f32 = lambda a: np.ascontiguousarray(np.asarray(a, dtype=np.float32))
    tr = lambda a: np.ascontiguousarray(np.asarray(a, dtype=np.float32).T)
    sw = lambda a: np.ascontiguousarray(np.asarray(a, dtype=np.float32).reshape(-1, 128).T)

    common = {
        "w_lh": f32(inputs["L0_attn_a_W_lh"]),
        "w_rh": f32(inputs["L0_attn_a_W_rh"]),
        "wqT": tr(inputs["L0_attn_a_Wq"]),
        "wvT": tr(inputs["L0_attn_a_Wv"]),
        "wcT": tr(inputs["L0_attn_a_Wc"]),
        "identv": sw(inputs["L0_attn_a_identity"]),
        "winT": tr(inputs["L0_attn_b_W_in"]),
        "waT": tr(inputs["L0_attn_b_W_alpha"]),
        "wbT": tr(inputs["L0_attn_b_W_beta"]),
        "wvbT": tr(inputs["L0_attn_b_W_v"]),
        "woT": tr(inputs["L0_attn_b_W_out"]),
        "b_al": sw(inputs["L0_attn_b_b_alpha"]),
        "b_be": sw(inputs["L0_attn_b_b_beta"]),
        "b_v": sw(inputs["L0_attn_b_b_v"]),
        "w1T": tr(inputs["L0_ffn_w1_w"]),
        "w2T": tr(inputs["L0_ffn_w2_w"]),
        "h1T": tr(inputs["L0_ffn_h1_w"]),
        "h2T": tr(inputs["L0_ffn_h2_w"]),
        "w3T": tr(inputs["L0_ffn_w3_w"]),
        "s1": sw(inputs["L0_ffn_w1_s"]),
        "s2": sw(inputs["L0_ffn_w2_s"]),
        "sh1": sw(inputs["L0_ffn_h1_s"]),
        "sh2": sw(inputs["L0_ffn_h2_s"]),
        "s3": sw(inputs["L0_ffn_w3_s"]),
    }
    in_maps = []
    for i in range(NCORES):
        xs = x[B * i:B * (i + 1)]
        xTc = np.ascontiguousarray(xs.transpose(2, 1, 0).reshape(C, TOK))
        m = dict(common)
        m["xT"] = xTc
        in_maps.append(m)

    nc = _get_prog(debug=False)
    res = run_bass_kernel_spmd(nc, in_maps, core_ids=list(range(NCORES)))
    outs = []
    for i in range(NCORES):
        o = res.results[i]["outT"]
        outs.append(o.reshape(C, T, B).transpose(2, 1, 0))
    return np.ascontiguousarray(np.concatenate(outs, axis=0))


# version 06 (final)
# speedup vs baseline: 1.2023x; 1.2023x over previous
import numpy as np
import ml_dtypes
import concourse.bass as bass
import concourse.mybir as mybir
from concourse.bass_utils import run_bass_kernel_spmd

F32 = mybir.dt.float32
BF16 = mybir.dt.bfloat16
I32 = mybir.dt.int32
AF = mybir.ActivationFunctionType
ALU = mybir.AluOpType

B = 2
T = 2048
C = 1024
H = 4096
DC = C // 128
HC = H // 128
TOK = B * T
SB = 512
NSB = TOK // SB
SCH = 256
NCH = T // SCH
LN_EPS = 1e-5
RMS_EPS = 1e-6
NS_ITERS = 12
NCORES = 8


class Cnt:
    __slots__ = ("h", "n")
    def __init__(self, handle):
        self.h = handle
        self.n = 0
    def inc(self, ins, k=1):
        ins.then_inc(self.h, k)
        self.n += k
        return ins


def build_program(debug=False):
    nc = bass.Bass("TRN2", target_bir_lowering=False)

    def din(name, shape):
        return nc.dram_tensor(name, shape, F32, kind="ExternalInput").ap()

    xT = din("xT", [C, TOK])
    w_lh = din("w_lh", [C, C])
    w_rh = din("w_rh", [C, C])
    wqT = din("wqT", [C, C])
    wvT = din("wvT", [C, C])
    wcT = din("wcT", [C, C])
    identv = din("identv", [128, DC])
    winT = din("winT", [C, C])
    waT = din("waT", [C, C])
    wbT = din("wbT", [C, C])
    wvbT = din("wvbT", [C, C])
    woT = din("woT", [C, C])
    b_al = din("b_al", [128, DC])
    b_be = din("b_be", [128, DC])
    b_v = din("b_v", [128, DC])
    w1T = din("w1T", [C, H])
    w2T = din("w2T", [C, H])
    h1T = din("h1T", [H, H])
    h2T = din("h2T", [H, H])
    w3T = din("w3T", [H, C])
    s1 = din("s1", [128, HC])
    s2 = din("s2", [128, HC])
    sh1 = din("sh1", [128, HC])
    sh2 = din("sh2", [128, HC])
    s3 = din("s3", [128, DC])

    outT = nc.dram_tensor("outT", [C, TOK], F32, kind="ExternalOutput").ap()

    def dint(name, shape, dt=F32):
        return nc.dram_tensor(name, shape, dt, kind="Internal").ap()

    x_res = dint("x_res", [C, TOK])
    w1Bd = dint("w1Bd", [HC, 128, DC, 128], BF16)
    w2Bd = dint("w2Bd", [HC, 128, DC, 128], BF16)
    h1Bd = dint("h1Bd", [HC, 128, HC, 128], BF16)
    h2Bd = dint("h2Bd", [HC, 128, HC, 128], BF16)
    w3Bd = dint("w3Bd", [DC, 128, HC, 128], BF16)
    wcBd = dint("wcBd", [DC, 128, DC, 128], BF16)
    winBd = dint("winBd", [DC, 128, DC, 128], BF16)
    waBd = dint("waBd", [DC, 128, DC, 128], BF16)
    wbBd = dint("wbBd", [DC, 128, DC, 128], BF16)
    wvBd = dint("wvBd", [DC, 128, DC, 128], BF16)
    woBd = dint("woBd", [DC, 128, DC, 128], BF16)
    qrD = dint("qrD", [C, TOK], BF16)
    qsD = dint("qsD", [C, TOK], BF16)
    vD = dint("vD", [C, TOK], BF16)

    dbg = {}
    if debug:
        for nm, shp in [("d_olh", [C, C]), ("d_orh", [C, C]),
                        ("d_x1", [C, TOK]), ("d_x2", [C, TOK]), ("d_x3", [C, TOK])]:
            dbg[nm] = nc.dram_tensor(nm, shp, F32, kind="ExternalOutput").ap()
        for nm, shp in [("d_qr", [C, TOK]), ("d_qs", [C, TOK])]:
            dbg[nm] = nc.dram_tensor(nm, shp, BF16, kind="ExternalOutput").ap()

    import contextlib
    es = contextlib.ExitStack()
    with es:
        ec = es.enter_context
        abf = ec(nc.sbuf_tensor("abf", [128, 32768], BF16))
        af32 = ec(nc.sbuf_tensor("af32", [128, 8192], F32))
        wslotA = ec(nc.sbuf_tensor("wslotA", [128, DC, C], BF16))
        wslotB = ec(nc.sbuf_tensor("wslotB", [128, DC, C], BF16))
        qr_s = ec(nc.sbuf_tensor("qr_s", [128, DC, 2 * SCH], BF16))
        qs_s = ec(nc.sbuf_tensor("qs_s", [128, DC, 2 * SCH], BF16))
        xn_s = ec(nc.sbuf_tensor("xn_s", [128, DC, SB], BF16))
        wbuf = ec(nc.sbuf_tensor("wbuf", [128, 2, HC, 128], BF16))
        x_t = ec(nc.sbuf_tensor("x_t", [128, DC, SB], F32))
        st = ec(nc.sbuf_tensor("st", [128, 2048], F32))
        sc_m = ec(nc.sbuf_tensor("sc", [128, 512], F32))
        scb = ec(nc.sbuf_tensor("scb", [128, 320], BF16))
        ones_f = ec(nc.sbuf_tensor("ones1", [128, 128], F32))
        idm = ec(nc.sbuf_tensor("idm", [128, 128], BF16))
        idmi = ec(nc.sbuf_tensor("idmi", [128, 128], I32))
        psA = ec(nc.psum_tensor("psA", [128, 4, 512], F32))
        psB = ec(nc.psum_tensor("psB", [128, 2, 512], F32))
        psU = ec(nc.psum_tensor("psU", [128, 2, 512], F32))
        _m_dma = ec(nc.semaphore("m_dma"))
        _m_dmo = ec(nc.semaphore("m_dmo"))
        _m_wst = ec(nc.semaphore("m_wst"))
        _m_pe = ec(nc.semaphore("m_pe"))
        _m_dve = ec(nc.semaphore("m_dve"))
        _m_act = ec(nc.semaphore("m_act"))
        _m_gp = ec(nc.semaphore("m_gp"))
        _m_sg0 = ec(nc.semaphore("m_sg0"))
        _m_sg1 = ec(nc.semaphore("m_sg1"))
        _m_w0 = ec(nc.semaphore("m_w0"))
        _m_w1 = ec(nc.semaphore("m_w1"))
        _m_so0 = ec(nc.semaphore("m_so0"))
        _m_so1 = ec(nc.semaphore("m_so1"))
        _m_ph = ec(nc.semaphore("m_ph"))
        _m_zpe = ec(nc.semaphore("m_zpe"))
        _m_zact = ec(nc.semaphore("m_zact"))
        _m_zdve = ec(nc.semaphore("m_zdve"))
        _m_zdma = ec(nc.semaphore("m_zdma"))
        _m_zso = ec(nc.semaphore("m_zso"))
        blk = ec(nc.Block())
        dma = Cnt(_m_dma)
        dmo = Cnt(_m_dmo)
        wst = Cnt(_m_wst)
        pe = Cnt(_m_pe)
        dve = Cnt(_m_dve)
        act = Cnt(_m_act)
        gp = Cnt(_m_gp)
        ph = Cnt(_m_ph)
        sgc = [Cnt(_m_sg0), Cnt(_m_sg1)]
        wbc = [Cnt(_m_w0), Cnt(_m_w1)]
        soc = [Cnt(_m_so0), Cnt(_m_so1)]
        zpe = Cnt(_m_zpe)
        zact = Cnt(_m_zact)
        zdve = Cnt(_m_zdve)
        zdma = Cnt(_m_zdma)
        zso = Cnt(_m_zso)

        v_ = lambda ap, a: ap.rearrange("p (a b) -> p a b", a=a)
        Xf = v_(af32[:, 0:8192], DC)
        Xbf = v_(abf[:, 0:8192], DC)
        XTb = v_(abf[:, 8192:16384], DC)
        Gb = v_(abf[:, 16384:24576], DC)
        wqB = v_(abf[:, 0:8192], DC)
        wvB = v_(abf[:, 8192:16384], DC)
        qTs = v_(abf[:, 16384:16384 + DC * SB], DC)
        vTs = v_(abf[:, 16384 + DC * SB:16384 + 2 * DC * SB], DC)
        qvs = v_(abf[:, 16384 + 2 * DC * SB:16384 + 3 * DC * SB], DC)
        hA = v_(abf[:, 0:16384], HC)
        hB = v_(abf[:, 16384:32768], HC)
        xpB = v_(abf[:, 0:4096], DC)
        cellB = v_(abf[:, 4096:8192], DC)
        wSlotC = v_(abf[:, 8192:16384], DC)
        wSlotD = v_(abf[:, 16384:24576], DC)
        wSlotE = v_(abf[:, 24576:32768], DC)
        alB = v_(af32[:, 0:4096], DC)
        bvB = v_(af32[:, 4096:8192], DC)
        stg32 = [af32[:, 0:4096], af32[:, 4096:8192]]
        stgbf = [abf[:, 0:4096], abf[:, 4096:8192]]
        z_f = v_(sc_m[:, 0:16], DC)
        zq_m = sc_m[:, 16:32]
        zq_v = v_(zq_m, DC)
        sg = sc_m[:, 32:34]
        sgq = sc_m[:, 34:36]
        rinv = sc_m[:, 36:38]
        z3 = [v_(sc_m[:, 160 + 16 * k:176 + 16 * k], DC) for k in range(3)]
        zqP = [sc_m[:, 208:224], sc_m[:, 224:240]]
        sgP = [sc_m[:, 240:242], sc_m[:, 242:244]]
        sgqP = [sc_m[:, 244:246], sc_m[:, 246:248]]
        rinvP = [sc_m[:, 248:250], sc_m[:, 250:252]]
        lnT = sc_m[:, 252:254]
        ybuf = [v_(sc_m[:, 272:288], DC), v_(sc_m[:, 288:304], DC)]
        c_mhalf = sc_m[:, 92:93]
        Dg = v_(sc_m[:, 256:272], 8)
        cb = [v_(scb[:, 32:160], DC), v_(scb[:, 160:288], DC)]
        carry = v_(sc_m[:, 48:64], DC)
        nrm = sc_m[:, 96:128]
        c_lneps = sc_m[:, 88:89]
        c_rmseps = sc_m[:, 89:90]
        c_invC = sc_m[:, 90:91]
        c_half = sc_m[:, 91:92]
        h0b = v_(scb[:, 0:16], DC)
        idtb = scb[:, 16:24]
        srw = st[:, 1536:2048]
        ln_tmp = v_(st[:, 0:1024], 2)
        ln_m = st[:, 1024:1536]
        BK = [psB[:, 1, :], psU[:, 0, :], psU[:, 1, :]]

        ENGS = ("sync", "gpsimd", "tensor", "vector", "scalar")

        def barrier():
            for eng in ENGS:
                def mk(e):
                    def f(x):
                        if e == "sync":
                            x.wait_ge(dma.h, dma.n)
                            x.wait_ge(wst.h, wst.n)
                            x.wait_ge(sgc[0].h, sgc[0].n)
                            x.wait_ge(sgc[1].h, sgc[1].n)
                            x.wait_ge(wbc[0].h, wbc[0].n)
                            x.wait_ge(wbc[1].h, wbc[1].n)
                            x.wait_ge(zdma.h, zdma.n)
                        if e == "gpsimd":
                            x.wait_ge(dmo.h, dmo.n)
                            x.wait_ge(soc[0].h, soc[0].n)
                            x.wait_ge(soc[1].h, soc[1].n)
                            x.wait_ge(zso.h, zso.n)
                        ph.inc(x.nop())
                    return f
                getattr(blk, eng)(mk(eng))
            target = ph.n
            for eng in ENGS:
                def mk2(e):
                    def f(x):
                        x.wait_ge(ph.h, target)
                    return f
                getattr(blk, eng)(mk2(eng))

        def load_wslot(src_ap, slot_view):
            s0 = [sgc[0].n, sgc[1].n]
            v0 = dve.n
            @blk.sync
            def _(sync):
                for half in range(2):
                    sync.wait_ge(dve.h, v0)
                    sgc[half].inc(sync.dma_start(
                        out=stg32[half].rearrange("p (a j) -> p a j", a=4),
                        in_=src_ap[half * 512:(half + 1) * 512, :]
                            .rearrange("(a p) j -> p a j", p=128)), 16)
            @blk.vector
            def _(v):
                for half in range(2):
                    v.wait_ge(sgc[half].h, s0[half] + 16)
                    dve.inc(v.tensor_copy(
                        slot_view[:, half * 4:(half + 1) * 4, :],
                        stg32[half].rearrange("p (a j) -> p a j", a=4)))

        def emit_ln(x_src, sb0):
            d0, a0, p0, v0 = dma.n, act.n, pe.n, dve.n
            @blk.sync
            def _(sync):
                sync.wait_ge(dmo.h, dmo.n)
                dma.inc(sync.dma_start(
                    out=x_t[:],
                    in_=x_src.rearrange("(a p) t -> p a t", p=128)[:, :, sb0:sb0 + SB]), 16)
            @blk.scalar
            def _(s):
                s.wait_ge(dma.h, d0 + 16)
                for dcc in range(DC):
                    if dcc >= 2:
                        s.wait_ge(pe.h, p0 + DC + dcc - 1)
                    act.inc(s.activation(out=ln_tmp[:, dcc % 2, :], in_=x_t[:, dcc, :],
                                         func=AF.Square))
            @blk.tensor
            def _(t):
                t.wait_ge(dma.h, d0 + 16)
                for dcc in range(DC):
                    ins = t.matmul(BK[0], ones_f[:], x_t[:, dcc, :],
                                   start=(dcc == 0), stop=(dcc == DC - 1))
                    pe.inc(ins)
                for dcc in range(DC):
                    t.wait_ge(act.h, a0 + dcc + 1)
                    ins = t.matmul(BK[1], ones_f[:], ln_tmp[:, dcc % 2, :],
                                   start=(dcc == 0), stop=(dcc == DC - 1))
                    pe.inc(ins)
            @blk.vector
            def _(v):
                v.wait_ge(pe.h, p0 + 2 * DC)
                v.tensor_scalar(out=ln_m[:], in0=BK[0], scalar1=1.0 / C,
                                scalar2=None, op0=ALU.mult)
                v.drain()
                dve.inc(v.tensor_mul(ln_tmp[:, 0, :], ln_m[:], ln_m[:]))
                v.drain()
                dve.inc(v.scalar_tensor_tensor(
                    out=ln_tmp[:, 1, :], in0=BK[1], scalar=1.0 / C,
                    in1=ln_tmp[:, 0, :], op0=ALU.mult, op1=ALU.subtract))
            @blk.scalar
            def _(s):
                s.wait_ge(dve.h, v0 + 2)
                act.inc(s.activation(out=ln_tmp[:, 0, :], in_=ln_tmp[:, 1, :],
                                     func=AF.Sqrt, bias=c_lneps))
            @blk.vector
            def _(v):
                v.wait_ge(act.h, a0 + DC + 1)
                dve.inc(v.reciprocal(ln_tmp[:, 1, :], ln_tmp[:, 0, :]))
                v.drain()
                for dcc in range(DC):
                    v.tensor_sub(ln_tmp[:, 0, :], x_t[:, dcc, :], ln_m[:])
                    v.drain()
                    ins = v.tensor_mul(xn_s[:, dcc, :], ln_tmp[:, 0, :], ln_tmp[:, 1, :])
                    v.drain()
                dve.inc(ins)

        def emit_proj8(slot_view, evac_fn, moving_view, wait_dve_ready):
            p0 = pe.n
            @blk.tensor
            def _(t):
                t.wait_ge(dve.h, wait_dve_ready)
                for mt in range(DC):
                    if mt >= 4:
                        t.wait_ge(act.h, evac_fn["a0"] + (mt - 3)) if evac_fn.get("on_act") \
                            else t.wait_ge(dve.h, evac_fn["v0"] + (mt - 3))
                    for kc in range(DC):
                        ins = t.matmul(psA[:, mt % 4, :],
                                       slot_view[:, kc, mt * 128:(mt + 1) * 128],
                                       moving_view[:, kc, :],
                                       start=(kc == 0), stop=(kc == DC - 1))
                    pe.inc(ins)
            return p0

        @blk.sync
        def _(sync):
            for apsrc, col, w in [(s1, 0, 32), (s2, 32, 32), (sh1, 64, 32),
                                  (sh2, 96, 32), (s3, 128, 8), (b_al, 136, 8),
                                  (b_be, 144, 8), (b_v, 152, 8)]:
                dma.inc(sync.dma_start(out=srw[:, col:col + w], in_=apsrc[:, :]), 16)
            dma.inc(sync.dma_start(out=sc_m[:, 40:48], in_=identv[:, :]), 16)
        d_const = dma.n

        @blk.gpsimd
        def _(g):
            gp.inc(g.iota(idmi[:], pattern=[[1, 128]], base=0, channel_multiplier=-1))

        @blk.vector
        def _(v):
            v.memset(ones_f[:], 1.0)
            v.memset(carry[:], 0.0)
            v.memset(c_lneps[:], LN_EPS)
            v.memset(c_rmseps[:], RMS_EPS)
            v.memset(c_invC[:], 1.0 / C)
            v.memset(c_half[:], 0.5)
            v.memset(c_mhalf[:], -0.5)
            v.wait_ge(gp.h, 1)
            v.tensor_single_scalar(out=idm[:], in_=idmi[:], scalar=0, op=ALU.is_equal)
            v.wait_ge(dma.h, d_const)
            v.tensor_copy(idtb[:], sc_m[:, 40:48])
            v.drain()
            for b in range(B):
                dve.inc(v.tensor_copy(h0b[:, :, b], idtb[:]))

        cast_jobs = []
        for src, dst, K, nmt in [(w1T, w1Bd, C, HC), (w2T, w2Bd, C, HC),
                                 (h1T, h1Bd, H, HC), (h2T, h2Bd, H, HC),
                                 (w3T, w3Bd, H, DC), (wcT, wcBd, C, DC),
                                 (winT, winBd, C, DC), (waT, waBd, C, DC),
                                 (wbT, wbBd, C, DC), (wvbT, wvBd, C, DC),
                                 (woT, woBd, C, DC)]:
            for kc in range(K // 128):
                cast_jobs.append((src, dst, nmt, kc))
        v0, o0 = dve.n, dmo.n
        s0c = [sgc[0].n, sgc[1].n]

        @blk.sync
        def _(sync):
            for i, (src, dst, nmt, kc) in enumerate(cast_jobs):
                if i >= 2:
                    sync.wait_ge(dve.h, v0 + (i - 1))
                sgc[i % 2].inc(sync.dma_start(out=stg32[i % 2][:, 0:nmt * 128],
                                              in_=src[kc * 128:(kc + 1) * 128, :]), 16)

        so0 = [soc[0].n, soc[1].n]

        @blk.vector
        def _(v):
            for i, (src, dst, nmt, kc) in enumerate(cast_jobs):
                v.wait_ge(sgc[i % 2].h, s0c[i % 2] + 16 * (i // 2 + 1))
                if i >= 2:
                    v.wait_ge(soc[i % 2].h, so0[i % 2] + 16 * ((i - 2) // 2 + 1))
                dve.inc(v.tensor_copy(stgbf[i % 2][:, 0:nmt * 128],
                                      stg32[i % 2][:, 0:nmt * 128]))

        @blk.gpsimd
        def _(g):
            for i, (src, dst, nmt, kc) in enumerate(cast_jobs):
                g.wait_ge(dve.h, v0 + (i + 1))
                soc[i % 2].inc(g.dma_start(
                    out=dst[:, :, kc, :].transpose([1, 0, 2]),
                    in_=stgbf[i % 2][:, 0:nmt * 128]
                        .rearrange("p (mt m) -> p mt m", m=128)), 16)

        barrier()

        def emit_polar(W_in, out_slot):
            d0 = dma.n
            @blk.sync
            def _(sync):
                dma.inc(sync.dma_start(
                    out=Xf[:], in_=W_in.rearrange("(a p) j -> p a j", p=128)), 16)

            a0, v0, p0 = act.n, dve.n, pe.n
            @blk.scalar
            def _(s):
                s.wait_ge(dma.h, d0 + 16)
                for j in range(2 * DC):
                    dcc, hh = j // 2, j % 2
                    if j >= 4:
                        s.wait_ge(dve.h, v0 + (j - 3))
                    act.inc(s.activation(out=x_t[:, j % 4, :],
                                         in_=Xf[:, dcc, hh * 512:(hh + 1) * 512],
                                         func=AF.Square))
            @blk.vector
            def _(v):
                for j in range(2 * DC):
                    v.wait_ge(act.h, a0 + j + 1)
                    dve.inc(v.tensor_reduce(out=nrm[:, j:j + 1],
                                            in_=x_t[:, j % 4, :],
                                            axis=mybir.AxisListType.X, op=ALU.add))
                v.drain()
                dve.inc(v.tensor_reduce(out=nrm[:, 16:17], in_=nrm[:, 0:16],
                                        axis=mybir.AxisListType.X, op=ALU.add))
            @blk.tensor
            def _(t):
                t.wait_ge(dve.h, v0 + 2 * DC + 1)
                pe.inc(t.matmul(psB[:, 0, 0:1], ones_f[:], nrm[:, 16:17],
                                start=True, stop=True))
            @blk.scalar
            def _(s):
                s.wait_ge(pe.h, p0 + 1)
                act.inc(s.activation(out=nrm[:, 17:18], in_=psB[:, 0, 0:1], func=AF.Sqrt))
            @blk.vector
            def _(v):
                v.wait_ge(act.h, a0 + 2 * DC + 1)
                v.tensor_scalar(out=nrm[:, 18:19], in0=nrm[:, 17:18],
                                scalar1=1e-8, scalar2=None, op0=ALU.add)
                v.drain()
                v.reciprocal(nrm[:, 19:20], nrm[:, 18:19])
                v.drain()
                for dcc in range(DC):
                    ins = v.tensor_scalar(out=Xf[:, dcc, :], in0=Xf[:, dcc, :],
                                          scalar1=nrm[:, 19:20], scalar2=None,
                                          op0=ALU.mult)
                dve.inc(ins)

            for it in range(NS_ITERS + 1):
                last = (it == NS_ITERS)
                vx = dve.n
                @blk.vector
                def _(v):
                    v.wait_ge(dve.h, vx)
                    v.drain()
                    for dcc in range(DC):
                        ins = v.tensor_copy(Xbf[:, dcc, :], Xf[:, dcc, :])
                    dve.inc(ins)
                vcast = dve.n
                p0t, v0t = pe.n, dve.n
                @blk.tensor
                def _(t):
                    t.wait_ge(dve.h, vcast)
                    k = 0
                    for ic in range(DC):
                        for jc in range(DC):
                            if k >= 4:
                                t.wait_ge(dve.h, v0t + (k - 3))
                            pe.inc(t.matmul(psA[:, k % 4, 0:128],
                                            Xbf[:, ic, jc * 128:(jc + 1) * 128],
                                            idm[:], start=True, stop=True))
                            k += 1
                @blk.vector
                def _(v):
                    k = 0
                    for ic in range(DC):
                        for jc in range(DC):
                            v.wait_ge(pe.h, p0t + k + 1)
                            dst = out_slot if last else XTb
                            dve.inc(v.tensor_copy(dst[:, jc, ic * 128:(ic + 1) * 128],
                                                  psA[:, k % 4, 0:128]))
                            k += 1
                if last:
                    break
                vtrans = dve.n
                p0g, a0g = pe.n, act.n
                @blk.tensor
                def _(t):
                    t.wait_ge(dve.h, vtrans)
                    k = 0
                    for mj in range(DC):
                        for nh in range(2):
                            if k >= 4:
                                t.wait_ge(act.h, a0g + (k - 3))
                            for kc in range(DC):
                                ins = t.matmul(psA[:, k % 4, :],
                                               Xbf[:, kc, mj * 128:(mj + 1) * 128],
                                               Xbf[:, kc, nh * 512:(nh + 1) * 512],
                                               start=(kc == 0), stop=(kc == DC - 1))
                            pe.inc(ins)
                            k += 1
                @blk.scalar
                def _(s):
                    k = 0
                    for mj in range(DC):
                        for nh in range(2):
                            s.wait_ge(pe.h, p0g + k + 1)
                            act.inc(s.activation(
                                out=Gb[:, mj, nh * 512:(nh + 1) * 512],
                                in_=psA[:, k % 4, :], func=AF.Copy, scale=c_half))
                            k += 1
                p0p, v0p = pe.n, dve.n
                @blk.tensor
                def _(t):
                    t.wait_ge(act.h, a0g + 16)
                    k = 0
                    for mi in range(DC):
                        for nh in range(2):
                            if k >= 4:
                                t.wait_ge(dve.h, v0p + (k - 3))
                            for kj in range(DC):
                                ins = t.matmul(psA[:, k % 4, :],
                                               XTb[:, kj, mi * 128:(mi + 1) * 128],
                                               Gb[:, kj, nh * 512:(nh + 1) * 512],
                                               start=(kj == 0), stop=(kj == DC - 1))
                            pe.inc(ins)
                            k += 1
                @blk.vector
                def _(v):
                    k = 0
                    for mi in range(DC):
                        for nh in range(2):
                            v.wait_ge(pe.h, p0p + k + 1)
                            dve.inc(v.scalar_tensor_tensor(
                                out=Xf[:, mi, nh * 512:(nh + 1) * 512],
                                in0=Xf[:, mi, nh * 512:(nh + 1) * 512],
                                scalar=1.5, in1=psA[:, k % 4, :],
                                op0=ALU.mult, op1=ALU.subtract))
                            k += 1

        emit_polar(w_lh, wslotA)
        barrier()
        emit_polar(w_rh, wslotB)
        barrier()

        def dump_dbg_slot(slot_view, dst_ap):
            if not debug:
                return
            v0d = dve.n
            @blk.vector
            def _(v):
                v.wait_ge(dmo.h, dmo.n)
                for dcc in range(DC):
                    ins = v.tensor_copy(Xf[:, dcc, :], slot_view[:, dcc, :])
                dve.inc(ins)
            @blk.gpsimd
            def _(g):
                g.wait_ge(dve.h, v0d + 1)
                dmo.inc(g.dma_start(out=dst_ap.rearrange("(a p) j -> p a j", p=128),
                                    in_=Xf[:]), 16)
            barrier()

        dump_dbg_slot(wslotA, dbg.get("d_olh"))
        dump_dbg_slot(wslotB, dbg.get("d_orh"))

        load_wslot(wqT, wqB)
        load_wslot(wvT, wvB)
        barrier()
        for sb in range(NSB):
            sb0 = sb * SB
            emit_ln(xT, sb0)
            vln = dve.n
            p0q, v0q = pe.n, dve.n
            @blk.tensor
            def _(t):
                t.wait_ge(dve.h, vln)
                k = 0
                for slotw in (wqB, wvB):
                    for mt in range(DC):
                        if k >= 4:
                            t.wait_ge(dve.h, v0q + (k - 3))
                        for kc in range(DC):
                            ins = t.matmul(psA[:, k % 4, :],
                                           slotw[:, kc, mt * 128:(mt + 1) * 128],
                                           xn_s[:, kc, :],
                                           start=(kc == 0), stop=(kc == DC - 1))
                        pe.inc(ins)
                        k += 1
            @blk.vector
            def _(v):
                k = 0
                for dst in (qTs, vTs):
                    for mt in range(DC):
                        v.wait_ge(pe.h, p0q + k + 1)
                        dve.inc(v.tensor_copy(dst[:, mt, :], psA[:, k % 4, :]))
                        k += 1
            vqv = dve.n
            p0r, v0r = pe.n, dve.n
            @blk.tensor
            def _(t):
                t.wait_ge(dve.h, vqv)
                for mt in range(DC):
                    if mt >= 4:
                        t.wait_ge(dve.h, v0r + (mt - 3))
                    for kc in range(DC):
                        ins = t.matmul(psA[:, mt % 4, :],
                                       wslotB[:, kc, mt * 128:(mt + 1) * 128],
                                       qTs[:, kc, :],
                                       start=(kc == 0), stop=(kc == DC - 1))
                    pe.inc(ins)
            @blk.vector
            def _(v):
                for mt in range(DC):
                    v.wait_ge(pe.h, p0r + mt + 1)
                    dve.inc(v.tensor_copy(qvs[:, mt, :], psA[:, mt % 4, :]))
            vst = dve.n
            @blk.gpsimd
            def _(g):
                g.wait_ge(dve.h, vst)
                dmo.inc(g.dma_start(
                    out=vD.rearrange("(a p) t -> p a t", p=128)[:, :, sb0:sb0 + SB],
                    in_=vTs[:]), 16)
                dmo.inc(g.dma_start(
                    out=qrD.rearrange("(a p) t -> p a t", p=128)[:, :, sb0:sb0 + SB],
                    in_=qvs[:]), 16)
        barrier()
        if debug:
            @blk.gpsimd
            def _(g):
                g.wait_ge(dmo.h, dmo.n)
                dmo.inc(g.dma_start(out=dbg["d_qr"], in_=qrD[:, :]), 16)
            barrier()

        RB = 8

        def gen_scan():
            @blk.vector
            def _(v):
                v.memset(cb[0][:], 0.0)
                v.memset(cb[1][:], 0.0)
                zdve.inc(v.memset(Dg[:], 1.0))
                v.drain()
            @blk.scalar
            def _(s):
                s.wait_ge(zdve.h, zdve.n)
                zact.inc(s.activation(out=cb[0][:, :, 0:2], in_=h0b[:], func=AF.Copy))
            pre_seed_act = zact.n
            yield None

            for ch in range(NCH):
                c0 = ch * 2 * SCH
                d0s = zdma.n
                @blk.sync
                def _(sync):
                    sync.wait_ge(zdve.h, zdve.n)
                    sync.wait_ge(zact.h, zact.n)
                    zdma.inc(sync.dma_start(
                        out=qr_s[:],
                        in_=qrD.rearrange("(a p) t -> p a t", p=128)[:, :, c0:c0 + 2 * SCH]), 16)
                o_pr = zso.n
                p0s, v0s, a0s = zpe.n, zdve.n, zact.n

                stats_m = {}
                passA_m = {}
                passB_m = {}
                n = p0s
                for tt in range(SCH):
                    if tt >= 1:
                        n += 1; stats_m[tt - 1] = n
                    n += 1; passA_m[tt] = n
                    n += 1; passB_m[tt] = n
                n += 1; stats_m[SCH - 1] = n
                pe_end = n

                evac_m = {}
                exp_m = {}
                n = a0s
                n += 1
                for tt in range(SCH):
                    if tt >= 1:
                        n += 1; exp_m[tt - 1] = n
                    n += 1; evac_m[tt] = n
                n += 1; exp_m[SCH - 1] = n
                act_end = n

                partB_m = {}
                zsq_m = {}
                apply_m = {}
                n = v0s
                for tt in range(SCH):
                    if tt >= 1:
                        n += 1; partB_m[tt - 1] = n
                    if tt >= 1 and tt % RB == 0:
                        n += 1; apply_m[tt - 1] = n
                    n += 1; zsq_m[tt] = n
                    if tt >= 1 and tt % RB != 0:
                        n += 1; apply_m[tt - 1] = n
                n += 1; partB_m[SCH - 1] = n
                n += 1; apply_m[SCH - 1] = n
                dve_end = n

                @blk.scalar
                def _(s):
                    s.wait_ge(zdma.h, d0s + 16)
                    s.wait_ge(zdve.h, v0s)
                    zact.inc(s.activation(out=cb[0][:, :, 2:4],
                                          in_=qr_s[:, :, 0:2], func=AF.Copy))
                yield None

                def emit_apply(v, ts):
                    j2 = ts % RB
                    v.wait_ge(zact.h, exp_m[ts])
                    v.drain()
                    for b in range(B):
                        v.tensor_scalar(out=qs_s[:, :, 2 * ts + b],
                                        in0=z3[ts % 3][:, :, b],
                                        scalar1=rinvP[ts % 2][:, b:b + 1],
                                        scalar2=None, op0=ALU.mult)
                    if j2 == RB - 1:
                        for b in range(B):
                            ins = v.tensor_scalar(out=cb[(ts + 1) % 2][:, :, b],
                                                  in0=z3[ts % 3][:, :, b],
                                                  scalar1=rinvP[ts % 2][:, b:b + 1],
                                                  scalar2=None, op0=ALU.mult)
                    else:
                        rb_ = rinvP[ts % 2].rearrange("p (a b) -> p a b", a=1).broadcast_to([128, 8, 2])
                        v.tensor_tensor(Dg[:], Dg[:], rb_, ALU.mult)
                        v.drain()
                        ins = v.memset(Dg[:, j2 + 1:j2 + 2, :], 1.0)
                    zdve.inc(ins)

                for tt in range(SCH):
                    @blk.tensor
                    def _(t):
                        if tt >= 1:
                            if tt >= 2:
                                t.wait_ge(zdve.h, partB_m[tt - 2])
                            t.wait_ge(zdve.h, zsq_m[tt - 1])
                            zpe.inc(t.matmul(psB[:, 0, 0:16], ones_f[:],
                                             zqP[(tt - 1) % 2], start=True, stop=True))
                        if tt == 0:
                            t.wait_ge(zdma.h, d0s + 16)
                            t.wait_ge(zact.h, a0s + 1)
                            if ch == 0:
                                t.wait_ge(zact.h, pre_seed_act)
                            t.wait_ge(zdve.h, v0s)
                        else:
                            t.wait_ge(zact.h, evac_m[tt - 1])
                            if tt % RB == 0:
                                t.wait_ge(zdve.h, apply_m[tt - 1])
                            if tt >= 3:
                                t.wait_ge(zdve.h, zsq_m[tt - 3])
                        zv = v_(psA[:, tt % 4, 0:128], DC)
                        for dcc in range(DC):
                            for kc in range(DC):
                                ins = t.matmul(zv[:, dcc, :],
                                               wslotA[:, kc, dcc * 128:(dcc + 1) * 128],
                                               cb[tt % 2][:, kc, :],
                                               start=(kc == 0), stop=(kc == DC - 1))
                            if dcc == 3:
                                zpe.inc(ins)
                        zpe.inc(ins)

                    @blk.scalar
                    def _(s):
                        j = tt % RB
                        zv = v_(psA[:, tt % 4, 0:128], DC)
                        if tt >= 1:
                            s.wait_ge(zdve.h, partB_m[tt - 1])
                            s.activation(out=lnT[:], in_=sgP[(tt - 1) % 2], func=AF.Ln,
                                         scale=c_invC, bias=c_rmseps)
                            s.drain()
                            zact.inc(s.activation(out=rinvP[(tt - 1) % 2], in_=lnT[:],
                                                  func=AF.Exp, scale=c_mhalf))
                        s.wait_ge(zpe.h, passB_m[tt])
                        ins = s.activation(out=cb[(tt + 1) % 2][:, :, :], in_=zv[:, :, :],
                                           func=AF.Copy)
                        if j <= 5 or (j == 7 and tt != SCH - 1):
                            s.drain()
                        if j <= 5:
                            si = j + 1
                            col = 2 * (tt - j + si)
                            ins = s.activation(out=cb[(tt + 1) % 2][:, :, 2 + 2 * si:4 + 2 * si],
                                               in_=qr_s[:, :, col:col + 2], func=AF.Copy)
                        elif j == 7 and tt != SCH - 1:
                            col = 2 * (tt + 1)
                            ins = s.activation(out=cb[(tt + 1) % 2][:, :, 2:4],
                                               in_=qr_s[:, :, col:col + 2], func=AF.Copy)
                        zact.inc(ins)

                    @blk.vector
                    def _(v):
                        j = tt % RB
                        zv = v_(psA[:, tt % 4, 0:128], DC)
                        zt = z3[tt % 3]
                        if tt >= 1:
                            v.wait_ge(zpe.h, stats_m[tt - 1])
                            zdve.inc(v.tensor_reduce(
                                out=sgP[(tt - 1) % 2],
                                in_=v_(psB[:, 0, 0:16], DC).transpose([0, 2, 1]),
                                axis=mybir.AxisListType.X, op=ALU.add))
                        if j == 0:
                            if tt >= 1:
                                emit_apply(v, tt - 1)
                            v.drain()
                            v.memset(Dg[:, 7:8, :], 1.0)
                            v.memset(Dg[:, 0:1, :], 1.0)
                        if j >= 1:
                            v.wait_ge(zpe.h, passB_m[tt])
                            v.wait_ge(zact.h, evac_m[tt])
                            yb = ybuf[tt % 2]
                            zvp = v_(psA[:, (tt - 1) % 4, 0:128], DC)
                            gamb = Dg[:, 7:8, :].broadcast_to([128, 8, 2])
                            v.drain()
                            v.tensor_tensor(yb[:], zv[:, :, 0:2], gamb, ALU.mult)
                            for i in range(j):
                                v.drain()
                                for b in range(B):
                                    v.scalar_tensor_tensor(
                                        out=yb[:, :, b], in0=zvp[:, :, 2 + 2 * i + b],
                                        scalar=Dg[:, i, b:b + 1], in1=yb[:, :, b],
                                        op0=ALU.mult, op1=ALU.add)
                            v.drain()
                        else:
                            v.wait_ge(zpe.h, passB_m[tt])
                            v.wait_ge(zact.h, evac_m[tt])
                        if tt == 0 and ch > 0:
                            v.wait_ge(zso.h, o_pr)
                        if j == 0:
                            v.tensor_add(zt[:], zv[:, :, 0:2], qr_s[:, :, 2 * tt:2 * tt + 2])
                            v.drain()
                        else:
                            v.wait_ge(zact.h, exp_m[tt - 1])
                            for b in range(B):
                                v.scalar_tensor_tensor(
                                    out=zt[:, :, b], in0=ybuf[tt % 2][:, :, b],
                                    scalar=rinvP[(tt - 1) % 2][:, b:b + 1],
                                    in1=qr_s[:, :, 2 * tt + b],
                                    op0=ALU.mult, op1=ALU.add)
                            v.drain()
                        zdve.inc(v.tensor_mul(v_(zqP[tt % 2], DC)[:], zt[:], zt[:]))
                        if tt >= 1 and tt % RB != 0:
                            emit_apply(v, tt - 1)
                    yield None

                @blk.tensor
                def _(t):
                    t.wait_ge(zdve.h, partB_m[SCH - 2])
                    t.wait_ge(zdve.h, zsq_m[SCH - 1])
                    zpe.inc(t.matmul(psB[:, 0, 0:16], ones_f[:],
                                     zqP[(SCH - 1) % 2], start=True, stop=True))
                @blk.scalar
                def _(s):
                    s.wait_ge(zdve.h, partB_m[SCH - 1])
                    s.activation(out=lnT[:], in_=sgP[(SCH - 1) % 2], func=AF.Ln,
                                 scale=c_invC, bias=c_rmseps)
                    s.drain()
                    zact.inc(s.activation(out=rinvP[(SCH - 1) % 2], in_=lnT[:],
                                          func=AF.Exp, scale=c_mhalf))
                @blk.vector
                def _(v):
                    v.wait_ge(zpe.h, stats_m[SCH - 1])
                    zdve.inc(v.tensor_reduce(
                        out=sgP[(SCH - 1) % 2],
                        in_=v_(psB[:, 0, 0:16], DC).transpose([0, 2, 1]),
                        axis=mybir.AxisListType.X, op=ALU.add))
                    v.wait_ge(zact.h, exp_m[SCH - 1])
                    ts = SCH - 1
                    for b in range(B):
                        v.tensor_scalar(out=qs_s[:, :, 2 * ts + b], in0=z3[ts % 3][:, :, b],
                                        scalar1=rinvP[ts % 2][:, b:b + 1], scalar2=None,
                                        op0=ALU.mult)
                    for b in range(B):
                        ins = v.tensor_scalar(out=cb[0][:, :, b], in0=z3[ts % 3][:, :, b],
                                              scalar1=rinvP[ts % 2][:, b:b + 1], scalar2=None,
                                              op0=ALU.mult)
                    zdve.inc(ins)
                @blk.gpsimd
                def _(g):
                    g.wait_ge(zdve.h, dve_end)
                    zso.inc(g.dma_start(
                        out=qsD.rearrange("(a p) t -> p a t", p=128)[:, :, c0:c0 + 2 * SCH],
                        in_=qs_s[:]), 16)
                yield ("chunk_done", ch)

        def qs_src(sb):
            sb0 = sb * SB
            return qsD.rearrange("(a p) t -> p a t", p=128)[:, :, sb0:sb0 + SB]

        def gen_p4(sb):
            sb0 = sb * SB
            d0y, v0y, p0y = dma.n, dve.n, pe.n
            wb0y = [wbc[0].n, wbc[1].n]
            pe_snap = pe.n
            @blk.sync
            def _(sync):
                sync.wait_ge(dmo.h, dmo.n)
                sync.wait_ge(zso.h, 16 * (sb + 1))
                dma.inc(sync.dma_start(
                    out=x_t[:],
                    in_=xT.rearrange("(a p) t -> p a t", p=128)[:, :, sb0:sb0 + SB]), 16)
                dma.inc(sync.dma_start(out=qTs[:], in_=qs_src(sb)), 16)
                dma.inc(sync.dma_start(
                    out=vTs[:],
                    in_=vD.rearrange("(a p) t -> p a t", p=128)[:, :, sb0:sb0 + SB]), 16)
            @blk.vector
            def _(v):
                v.wait_ge(dma.h, d0y + 48)
                dve.inc(v.tensor_mul(qvs[:], qTs[:], vTs[:]))
            yield None
            for mt in range(DC):
                @blk.sync
                def _(sync):
                    if mt == 0:
                        sync.wait_ge(pe.h, pe_snap)
                    if mt >= 2:
                        sync.wait_ge(pe.h, p0y + (mt - 1))
                    wbc[mt % 2].inc(sync.dma_start(
                        out=wbuf[:, mt % 2, 0:DC, :], in_=wcBd[mt]), 16)
                @blk.tensor
                def _(t):
                    if mt == 0:
                        t.wait_ge(dve.h, v0y + 1)
                    t.wait_ge(wbc[mt % 2].h, wb0y[mt % 2] + 16 * (mt // 2 + 1))
                    if mt >= 3:
                        t.wait_ge(dve.h, v0y + 1 + (mt - 2))
                    for kc in range(DC):
                        ins = t.matmul(BK[mt % 3],
                                       wbuf[:, mt % 2, kc, :],
                                       qvs[:, kc, :],
                                       start=(kc == 0), stop=(kc == DC - 1))
                    pe.inc(ins)
                @blk.vector
                def _(v):
                    v.wait_ge(pe.h, p0y + mt + 1)
                    dve.inc(v.tensor_add(x_t[:, mt, :], BK[mt % 3], x_t[:, mt, :]))
                yield None
            vy = dve.n
            @blk.gpsimd
            def _(g):
                g.wait_ge(dve.h, vy)
                dmo.inc(g.dma_start(
                    out=x_res.rearrange("(a p) t -> p a t", p=128)[:, :, sb0:sb0 + SB],
                    in_=x_t[:]), 16)
            yield None

        def gen_ffn(sb, x_src, x_dst):
            sb0 = sb * SB
            emit_ln(x_src, sb0)
            yield None
            vln = dve.n
            p0, a0, v0f = pe.n, act.n, dve.n
            pe_snap = pe.n
            wb0 = [wbc[0].n, wbc[1].n]
            for mt in range(HC):
                @blk.sync
                def _(sync):
                    if mt == 0:
                        sync.wait_ge(pe.h, pe_snap)
                    if mt >= 2:
                        sync.wait_ge(pe.h, p0 + (mt - 1))
                    wbc[mt % 2].inc(sync.dma_start(
                        out=wbuf[:, mt % 2, 0:DC, :], in_=w1Bd[mt]), 16)
                    wbc[mt % 2].inc(sync.dma_start(
                        out=wbuf[:, mt % 2, DC:2 * DC, :], in_=w2Bd[mt]), 16)
                @blk.tensor
                def _(t):
                    if mt == 0:
                        t.wait_ge(dve.h, vln)
                    t.wait_ge(wbc[mt % 2].h, wb0[mt % 2] + 32 * (mt // 2 + 1))
                    if mt >= 2:
                        t.wait_ge(dve.h, v0f + (mt - 1))
                    for kc in range(DC):
                        t.matmul(BK[(2 * mt) % 3],
                                 wbuf[:, mt % 2, kc, :], xn_s[:, kc, :],
                                 start=(kc == 0), stop=(kc == DC - 1))
                    if mt >= 1:
                        t.wait_ge(act.h, a0 + mt)
                    for kc in range(DC):
                        ins = t.matmul(BK[(2 * mt + 1) % 3],
                                       wbuf[:, mt % 2, DC + kc, :], xn_s[:, kc, :],
                                       start=(kc == 0), stop=(kc == DC - 1))
                    pe.inc(ins)
                @blk.scalar
                def _(s):
                    s.wait_ge(pe.h, p0 + mt + 1)
                    if mt >= 2:
                        s.wait_ge(dve.h, v0f + (mt - 1))
                    act.inc(s.activation(out=ln_tmp[:, mt % 2, :],
                                         in_=BK[(2 * mt) % 3], func=AF.Silu,
                                         scale=srw[:, mt:mt + 1]))
                @blk.vector
                def _(v):
                    v.wait_ge(act.h, a0 + mt + 1)
                    dve.inc(v.scalar_tensor_tensor(
                        out=hA[:, mt, :], in0=BK[(2 * mt + 1) % 3],
                        scalar=srw[:, 32 + mt:32 + mt + 1],
                        in1=ln_tmp[:, mt % 2, :], op0=ALU.mult, op1=ALU.mult))
                yield None
            vw12 = dve.n

            p1h, a1h = pe.n, act.n
            wb1 = [wbc[0].n, wbc[1].n]
            for mt in range(HC):
                @blk.sync
                def _(sync):
                    if mt == 0:
                        sync.wait_ge(pe.h, p1h)
                    if mt >= 2:
                        sync.wait_ge(pe.h, p1h + (mt - 1))
                    wbc[mt % 2].inc(sync.dma_start(out=wbuf[:, mt % 2, :, :], in_=h1Bd[mt]), 16)
                @blk.tensor
                def _(t):
                    if mt == 0:
                        t.wait_ge(dve.h, vw12)
                    t.wait_ge(wbc[mt % 2].h, wb1[mt % 2] + 16 * (mt // 2 + 1))
                    if mt >= 3:
                        t.wait_ge(act.h, a1h + (mt - 2))
                    for kc in range(HC):
                        ins = t.matmul(BK[mt % 3],
                                       wbuf[:, mt % 2, kc, :], hA[:, kc, :],
                                       start=(kc == 0), stop=(kc == HC - 1))
                    pe.inc(ins)
                @blk.scalar
                def _(s):
                    s.wait_ge(pe.h, p1h + mt + 1)
                    act.inc(s.activation(out=hB[:, mt, :], in_=BK[mt % 3],
                                         func=AF.Gelu,
                                         scale=srw[:, 64 + mt:64 + mt + 1]))
                yield None
            a_h1 = act.n

            p2h, a2h = pe.n, act.n
            wb2 = [wbc[0].n, wbc[1].n]
            for mt in range(HC):
                @blk.sync
                def _(sync):
                    if mt == 0:
                        sync.wait_ge(pe.h, p2h)
                    if mt >= 2:
                        sync.wait_ge(pe.h, p2h + (mt - 1))
                    wbc[mt % 2].inc(sync.dma_start(out=wbuf[:, mt % 2, :, :], in_=h2Bd[mt]), 16)
                @blk.tensor
                def _(t):
                    if mt == 0:
                        t.wait_ge(act.h, a_h1)
                    t.wait_ge(wbc[mt % 2].h, wb2[mt % 2] + 16 * (mt // 2 + 1))
                    if mt >= 3:
                        t.wait_ge(act.h, a2h + (mt - 2))
                    for kc in range(HC):
                        ins = t.matmul(BK[mt % 3],
                                       wbuf[:, mt % 2, kc, :], hB[:, kc, :],
                                       start=(kc == 0), stop=(kc == HC - 1))
                    pe.inc(ins)
                @blk.scalar
                def _(s):
                    s.wait_ge(pe.h, p2h + mt + 1)
                    act.inc(s.activation(out=hA[:, mt, :], in_=BK[mt % 3],
                                         func=AF.Gelu,
                                         scale=srw[:, 96 + mt:96 + mt + 1]))
                yield None
            a_h2 = act.n

            p3h, v3h = pe.n, dve.n
            wb3 = [wbc[0].n, wbc[1].n]
            for mt in range(DC):
                @blk.sync
                def _(sync):
                    if mt == 0:
                        sync.wait_ge(pe.h, p3h)
                    if mt >= 2:
                        sync.wait_ge(pe.h, p3h + (mt - 1))
                    wbc[mt % 2].inc(sync.dma_start(out=wbuf[:, mt % 2, :, :], in_=w3Bd[mt]), 16)
                @blk.tensor
                def _(t):
                    if mt == 0:
                        t.wait_ge(act.h, a_h2)
                    t.wait_ge(wbc[mt % 2].h, wb3[mt % 2] + 16 * (mt // 2 + 1))
                    if mt >= 3:
                        t.wait_ge(dve.h, v3h + (mt - 2))
                    for kc in range(HC):
                        ins = t.matmul(BK[mt % 3],
                                       wbuf[:, mt % 2, kc, :], hA[:, kc, :],
                                       start=(kc == 0), stop=(kc == HC - 1))
                    pe.inc(ins)
                @blk.vector
                def _(v):
                    v.wait_ge(pe.h, p3h + mt + 1)
                    dve.inc(v.scalar_tensor_tensor(
                        out=x_t[:, mt, :], in0=BK[mt % 3],
                        scalar=srw[:, 128 + mt:128 + mt + 1],
                        in1=x_t[:, mt, :], op0=ALU.mult, op1=ALU.add))
                yield None
            vres = dve.n
            @blk.gpsimd
            def _(g):
                g.wait_ge(dve.h, vres)
                dmo.inc(g.dma_start(
                    out=x_dst.rearrange("(a p) t -> p a t", p=128)[:, :, sb0:sb0 + SB],
                    in_=x_t[:]), 16)
            yield None

        def gen_attnb(sb):
            sb0 = sb * SB
            emit_ln(x_res, sb0)
            yield None
            vln = dve.n
            p0x, a0x = pe.n, act.n
            wb0 = [wbc[0].n, wbc[1].n]
            pe_snap = pe.n
            for mt in range(DC):
                @blk.sync
                def _(sync):
                    if mt == 0:
                        sync.wait_ge(pe.h, pe_snap)
                    if mt >= 2:
                        sync.wait_ge(pe.h, p0x + (mt - 1))
                    wbc[mt % 2].inc(sync.dma_start(
                        out=wbuf[:, mt % 2, 0:DC, :], in_=winBd[mt]), 16)
                @blk.tensor
                def _(t):
                    if mt == 0:
                        t.wait_ge(dve.h, vln)
                    t.wait_ge(wbc[mt % 2].h, wb0[mt % 2] + 16 * (mt // 2 + 1))
                    if mt >= 3:
                        t.wait_ge(act.h, a0x + (mt - 2))
                    for kc in range(DC):
                        ins = t.matmul(BK[mt % 3],
                                       wbuf[:, mt % 2, kc, :],
                                       xn_s[:, kc, :],
                                       start=(kc == 0), stop=(kc == DC - 1))
                    pe.inc(ins)
                @blk.scalar
                def _(s):
                    s.wait_ge(pe.h, p0x + mt + 1)
                    act.inc(s.activation(out=xpB[:, mt, :], in_=BK[mt % 3],
                                         func=AF.Silu))
                yield None
            a_xp = act.n
            p0a, a0a = pe.n, act.n
            wb1 = [wbc[0].n, wbc[1].n]
            pe_snap2 = pe.n
            for mt in range(DC):
                @blk.sync
                def _(sync):
                    if mt == 0:
                        sync.wait_ge(pe.h, pe_snap2)
                    if mt >= 2:
                        sync.wait_ge(pe.h, p0a + (mt - 1))
                    wbc[mt % 2].inc(sync.dma_start(
                        out=wbuf[:, mt % 2, 0:DC, :], in_=waBd[mt]), 16)
                @blk.tensor
                def _(t):
                    if mt == 0:
                        t.wait_ge(act.h, a_xp)
                    t.wait_ge(wbc[mt % 2].h, wb1[mt % 2] + 16 * (mt // 2 + 1))
                    if mt >= 3:
                        t.wait_ge(act.h, a0a + (mt - 2))
                    for kc in range(DC):
                        ins = t.matmul(BK[mt % 3],
                                       wbuf[:, mt % 2, kc, :],
                                       xpB[:, kc, :],
                                       start=(kc == 0), stop=(kc == DC - 1))
                    pe.inc(ins)
                @blk.scalar
                def _(s):
                    s.wait_ge(pe.h, p0a + mt + 1)
                    act.inc(s.activation(out=alB[:, mt, :], in_=BK[mt % 3],
                                         func=AF.Sigmoid,
                                         bias=srw[:, 136 + mt:136 + mt + 1]))
                yield None
            p0b, a0b, v0b = pe.n, act.n, dve.n
            wb2 = [wbc[0].n, wbc[1].n]
            pe_snap3 = pe.n
            for mt in range(DC):
                @blk.sync
                def _(sync):
                    if mt == 0:
                        sync.wait_ge(pe.h, pe_snap3)
                    if mt >= 2:
                        sync.wait_ge(pe.h, p0b + (mt - 1))
                    wbc[mt % 2].inc(sync.dma_start(
                        out=wbuf[:, mt % 2, 0:DC, :], in_=wbBd[mt]), 16)
                    wbc[mt % 2].inc(sync.dma_start(
                        out=wbuf[:, mt % 2, DC:2 * DC, :], in_=wvBd[mt]), 16)
                @blk.tensor
                def _(t):
                    if mt == 0:
                        t.wait_ge(act.h, a0a + DC)
                    t.wait_ge(wbc[mt % 2].h, wb2[mt % 2] + 32 * (mt // 2 + 1))
                    if mt >= 1:
                        t.wait_ge(dve.h, v0b + mt)
                    for kc in range(DC):
                        t.matmul(BK[0],
                                 wbuf[:, mt % 2, kc, :],
                                 xpB[:, kc, :],
                                 start=(kc == 0), stop=(kc == DC - 1))
                    for kc in range(DC):
                        ins = t.matmul(BK[1],
                                       wbuf[:, mt % 2, DC + kc, :],
                                       xpB[:, kc, :],
                                       start=(kc == 0), stop=(kc == DC - 1))
                    pe.inc(ins)
                @blk.scalar
                def _(s):
                    s.wait_ge(pe.h, p0b + mt + 1)
                    if mt >= 1:
                        s.wait_ge(dve.h, v0b + mt)
                    s.activation(out=ln_tmp[:, 0, :], in_=BK[0],
                                 func=AF.Sigmoid, bias=srw[:, 144 + mt:144 + mt + 1])
                    act.inc(s.activation(out=ln_tmp[:, 1, :], in_=BK[1],
                                         func=AF.Tanh,
                                         bias=srw[:, 152 + mt:152 + mt + 1]))
                @blk.vector
                def _(v):
                    v.wait_ge(act.h, a0b + mt + 1)
                    dve.inc(v.tensor_mul(bvB[:, mt, :], ln_tmp[:, 0, :], ln_tmp[:, 1, :]))
                yield None
            v0sc = dve.n
            for dcc in range(DC):
                @blk.vector
                def _(v):
                    if dcc == 0:
                        v.wait_ge(dve.h, v0sc)
                        v.drain()
                    al_tb = alB[:, dcc, :].rearrange("p (t b) -> p b t", b=B)
                    bv_tb = bvB[:, dcc, :].rearrange("p (t b) -> p b t", b=B)
                    for b in range(B):
                        v.tensor_tensor_scan(
                            out=al_tb[:, b, :], data0=al_tb[:, b, :],
                            data1=bv_tb[:, b, :],
                            initial=carry[:, dcc, b:b + 1],
                            op0=ALU.mult, op1=ALU.add)
                        v.drain()
                        ins = v.tensor_copy(carry[:, dcc, b:b + 1],
                                            al_tb[:, b, SB // B - 1:SB // B])
                    if dcc == DC - 1:
                        dve.inc(ins)
                yield None
            v_scan = dve.n
            a0c, v0c = act.n, dve.n
            @blk.scalar
            def _(s):
                s.wait_ge(dve.h, v_scan)
                for dcc in range(DC):
                    if dcc >= 2:
                        s.wait_ge(dve.h, v0c + (dcc - 1))
                    act.inc(s.activation(out=ln_tmp[:, dcc % 2, :],
                                         in_=alB[:, dcc, :], func=AF.Silu))
            @blk.vector
            def _(v):
                v.drain()
                for dcc in range(DC):
                    v.wait_ge(act.h, a0c + dcc + 1)
                    dve.inc(v.tensor_mul(cellB[:, dcc, :], alB[:, dcc, :],
                                         ln_tmp[:, dcc % 2, :]))
            yield None
            v_cell = dve.n
            p0o, v0o = pe.n, dve.n
            wb3 = [wbc[0].n, wbc[1].n]
            pe_snap4 = pe.n
            for mt in range(DC):
                @blk.sync
                def _(sync):
                    if mt == 0:
                        sync.wait_ge(pe.h, pe_snap4)
                    if mt >= 2:
                        sync.wait_ge(pe.h, p0o + (mt - 1))
                    wbc[mt % 2].inc(sync.dma_start(
                        out=wbuf[:, mt % 2, 0:DC, :], in_=woBd[mt]), 16)
                @blk.tensor
                def _(t):
                    if mt == 0:
                        t.wait_ge(dve.h, v_cell)
                    t.wait_ge(wbc[mt % 2].h, wb3[mt % 2] + 16 * (mt // 2 + 1))
                    if mt >= 3:
                        t.wait_ge(dve.h, v0o + (mt - 2))
                    for kc in range(DC):
                        ins = t.matmul(BK[mt % 3],
                                       wbuf[:, mt % 2, kc, :],
                                       cellB[:, kc, :],
                                       start=(kc == 0), stop=(kc == DC - 1))
                    pe.inc(ins)
                @blk.vector
                def _(v):
                    v.wait_ge(pe.h, p0o + mt + 1)
                    dve.inc(v.tensor_add(x_t[:, mt, :], BK[mt % 3], x_t[:, mt, :]))
                yield None
            v_ob = dve.n
            @blk.gpsimd
            def _(g):
                g.wait_ge(dve.h, v_ob)
                dmo.inc(g.dma_start(
                    out=x_res.rearrange("(a p) t -> p a t", p=128)[:, :, sb0:sb0 + SB],
                    in_=x_t[:]), 16)
            yield None

        def gen_bulk():
            for sb in range(NSB):
                yield ("need_scan", sb)
                for q in gen_p4(sb):
                    yield q
                for q in gen_ffn(sb, x_res, x_res):
                    yield q
                for q in gen_attnb(sb):
                    yield q
                for q in gen_ffn(sb, x_res, outT):
                    yield q

        gs = gen_scan()
        gb = gen_bulk()
        scan_state = {"chunks": 0, "done": False}

        def pull_scan():
            try:
                tok = next(gs)
                if isinstance(tok, tuple):
                    scan_state["chunks"] = tok[1] + 1
            except StopIteration:
                scan_state["done"] = True

        acc = 0.0
        for tok in gb:
            if isinstance(tok, tuple):
                need = tok[1] + 1
                while not scan_state["done"] and scan_state["chunks"] < need:
                    pull_scan()
            else:
                acc += 1.0
                while acc >= 1.0 and not scan_state["done"]:
                    pull_scan()
                    acc -= 1.0
        while not scan_state["done"]:
            pull_scan()
        barrier()

    return nc


_PROG_CACHE = {}


def _get_prog(debug=False):
    key = debug
    if key not in _PROG_CACHE:
        _PROG_CACHE[key] = build_program(debug=debug)
    return _PROG_CACHE[key]


def kernel(**inputs):
    x = np.asarray(inputs["L0_attn_a_x"], dtype=np.float32)
    f32 = lambda a: np.ascontiguousarray(np.asarray(a, dtype=np.float32))
    tr = lambda a: np.ascontiguousarray(np.asarray(a, dtype=np.float32).T)
    sw = lambda a: np.ascontiguousarray(np.asarray(a, dtype=np.float32).reshape(-1, 128).T)

    common = {
        "w_lh": f32(inputs["L0_attn_a_W_lh"]),
        "w_rh": f32(inputs["L0_attn_a_W_rh"]),
        "wqT": tr(inputs["L0_attn_a_Wq"]),
        "wvT": tr(inputs["L0_attn_a_Wv"]),
        "wcT": tr(inputs["L0_attn_a_Wc"]),
        "identv": sw(inputs["L0_attn_a_identity"]),
        "winT": tr(inputs["L0_attn_b_W_in"]),
        "waT": tr(inputs["L0_attn_b_W_alpha"]),
        "wbT": tr(inputs["L0_attn_b_W_beta"]),
        "wvbT": tr(inputs["L0_attn_b_W_v"]),
        "woT": tr(inputs["L0_attn_b_W_out"]),
        "b_al": sw(inputs["L0_attn_b_b_alpha"]),
        "b_be": sw(inputs["L0_attn_b_b_beta"]),
        "b_v": sw(inputs["L0_attn_b_b_v"]),
        "w1T": tr(inputs["L0_ffn_w1_w"]),
        "w2T": tr(inputs["L0_ffn_w2_w"]),
        "h1T": tr(inputs["L0_ffn_h1_w"]),
        "h2T": tr(inputs["L0_ffn_h2_w"]),
        "w3T": tr(inputs["L0_ffn_w3_w"]),
        "s1": sw(inputs["L0_ffn_w1_s"]),
        "s2": sw(inputs["L0_ffn_w2_s"]),
        "sh1": sw(inputs["L0_ffn_h1_s"]),
        "sh2": sw(inputs["L0_ffn_h2_s"]),
        "s3": sw(inputs["L0_ffn_w3_s"]),
    }
    in_maps = []
    for i in range(NCORES):
        xs = x[B * i:B * (i + 1)]
        xTc = np.ascontiguousarray(xs.transpose(2, 1, 0).reshape(C, TOK))
        m = dict(common)
        m["xT"] = xTc
        in_maps.append(m)

    nc = _get_prog(debug=False)
    res = run_bass_kernel_spmd(nc, in_maps, core_ids=list(range(NCORES)))
    outs = []
    for i in range(NCORES):
        o = res.results[i]["outT"]
        outs.append(o.reshape(C, T, B).transpose(2, 1, 0))
    return np.ascontiguousarray(np.concatenate(outs, axis=0))
